# version 03
# baseline (speedup 1.0000x reference)
import numpy as np
import ml_dtypes
import concourse.bass as bass
import concourse.mybir as mybir
from concourse.bass_utils import run_bass_kernel_spmd

F32 = mybir.dt.float32
BF16 = mybir.dt.bfloat16
U8 = mybir.dt.uint8
AF = mybir.ActivationFunctionType
ALU = mybir.AluOpType

D = 2048
NCH = 16
TOK = 2048
HALO = 128
WEXT = TOK + HALO
EPS = 1e-5
LN1E8 = 18.420680743952367
ESCALE = 0.08838834764831845
NEGBIG = -1e9

EXT_TILES = [(0, 512), (512, 512), (1024, 512), (1536, 512), (2048, 128)]
OWN_TILES = [(0, 512), (512, 512), (1024, 512), (1536, 512)]
HALVES = [[(0, 512), (512, 512), (1024, 128)], [(1152, 512), (1664, 512)]]
HALF_RANGE = [(0, 1152), (1152, 1024)]

TS_ATT = [20, 24, 28, 32]
COLOFF = [0, 16, 36, 60]

ENGINES = ("sync", "tensor", "vector", "scalar", "gpsimd")


class Sem:
    __slots__ = ("h", "n", "name")
    def __init__(self, nc, name):
        self.h = nc.alloc_semaphore(name)
        self.n = 0
        self.name = name


class KB:
    def __init__(self, nc):
        self.nc = nc
        self.ops = {e: [] for e in ENGINES}
        self._sid = 0

    def sem(self, name):
        self._sid += 1
        return Sem(self.nc, f"{name}{self._sid}")

    def emit(self, eng, fn):
        self.ops[eng].append(fn)

    def wait(self, eng, sem, n=None):
        n = sem.n if n is None else n
        if n > 0:
            self.emit(eng, lambda e, h=sem.h, n=n: e.wait_ge(h, n))
        return n

    def op(self, eng, build, sig=None, k=1):
        if sig is None:
            self.emit(eng, build)
            return None
        sig.n += k
        n = sig.n
        self.emit(eng, lambda e, b=build, h=sig.h, k=k: b(e).then_inc(h, k))
        return n

    def dma(self, eng, dst, src, sig):
        return self.op(eng, lambda e, d=dst, s=src: e.dma_start(out=d, in_=s), sig, 16)

    def finalize(self):
        with self.nc.Block() as blk:
            for eng in ENGINES:
                lst = self.ops[eng]
                getattr(blk, eng)(lambda e, lst=lst: [f(e) for f in lst])



def _u8(w):
    w = np.ascontiguousarray(np.asarray(w, dtype=np.int32))
    return w.view(np.uint8).reshape(*w.shape, 4)[..., 0]

def _stageT(w):
    dout, c = w.shape
    S, A = dout // 128, c // 128
    return np.ascontiguousarray(w.reshape(S, 128, A, 128).transpose(0, 3, 2, 1))

def _stageN(w):
    c, dout = w.shape
    A, S = c // 128, dout // 128
    return np.ascontiguousarray(w.reshape(A, 128, S, 128).transpose(2, 1, 0, 3))

def _veccol(v, nch=NCH):
    return np.ascontiguousarray(np.asarray(v, dtype=np.float32).reshape(nch, 128).T)



class Core:
    def __init__(self, nc, tap=None):
        self.nc = nc
        self.kb = KB(nc)
        self.tap = tap
        self._declare_io()
        self._alloc()

    def _declare_io(self):
        nc = self.nc
        dt_ = nc.dram_tensor
        self.x_ext = dt_("x_ext", [D, WEXT], F32, kind="ExternalInput")
        self.w_tmr = dt_("w_tmr", [16, 128, 16, 128], F32, kind="ExternalInput")
        self.w_tmk = dt_("w_tmk", [16, 128, 16, 128], F32, kind="ExternalInput")
        self.w_tmv = dt_("w_tmv", [16, 128, 16, 128], F32, kind="ExternalInput")
        self.w_tmo = dt_("w_tmo", [16, 128, 16, 128], F32, kind="ExternalInput")
        self.w_cmr = dt_("w_cmr", [16, 128, 16, 128], F32, kind="ExternalInput")
        self.w_cmk = dt_("w_cmk", [16, 128, 16, 128], F32, kind="ExternalInput")
        self.w_cmv = dt_("w_cmv", [16, 128, 16, 128], F32, kind="ExternalInput")
        self.w_lat = dt_("w_lat", [1, 128, 16, 128], F32, kind="ExternalInput")
        self.w_d = dt_("w_d", [16, 128, 16, 128], F32, kind="ExternalInput")
        self.w_proj = dt_("w_proj", [16, 128, 16, 128], F32, kind="ExternalInput")
        self.w_gate = dt_("w_gate", [64, 128, 16, 128], U8, kind="ExternalInput")
        self.w_up = dt_("w_up", [64, 128, 16, 128], U8, kind="ExternalInput")
        self.w_down = dt_("w_down", [16, 128, 64, 128], U8, kind="ExternalInput")
        for nm, sh in [("v_ln1g", 16), ("v_ln1b", 16), ("v_ln2g", 16), ("v_ln2b", 16),
                       ("v_u", 16), ("v_mur", 16), ("v_muk", 16), ("v_muv", 16),
                       ("v_cmur", 16), ("v_cmuk", 16), ("v_gsc", 64), ("v_usc", 64),
                       ("v_dsc", 16), ("v_bd", 16), ("v_bproj", 16)]:
            setattr(self, nm, dt_(nm, [128, sh], F32, kind="ExternalInput"))
        self.v_blat = dt_("v_blat", [128, 1], F32, kind="ExternalInput")
        self.v_hmask = dt_("v_hmask", [128, 1], F32, kind="ExternalInput")
        self.v_ident = dt_("v_ident", [128, 128], BF16, kind="ExternalInput")
        self.v_tri = dt_("v_tri", [128, 2048], BF16, kind="ExternalInput")
        self.v_btab = dt_("v_btab", [128, 89], F32, kind="ExternalInput")
        self.tmout_d = dt_("tmout_d", [D, WEXT], F32, kind="Internal")
        self.xb_d = dt_("xb_d", [D, WEXT], F32, kind="Internal")
        self.x1_d = dt_("x1_d", [D, TOK], F32, kind="Internal")
        self.x2_d = dt_("x2_d", [D, TOK], F32, kind="Internal")
        self.x3_d = dt_("x3_d", [D, TOK], F32, kind="Internal")
        self.rows_d = dt_("rows_d", [2, WEXT], F32, kind="Internal")
        self.kcc_in = dt_("kcc_in", [128, TOK], BF16, kind="Internal")
        self.kcc_out = dt_("kcc_out", [256, TOK], BF16, kind="Internal")
        self.out = dt_("out", [D, TOK], F32, kind="ExternalOutput")
        if self.tap:
            shp = {"h1": [D, WEXT], "tmout": [D, WEXT], "xb": [D, WEXT],
                   "h2": [D, WEXT], "x1": [D, TOK], "x2": [D, TOK],
                   "x3": [D, TOK], "qk": [D + 128, TOK],
                   "yatt": [D, TOK], "kfull": [256, 4096]}[self.tap]
            self.tap_d = dt_("tap_d", shp, F32, kind="ExternalOutput")

    def _alloc(self):
        nc = self.nc
        A = nc.alloc_sbuf_tensor
        for nm, w in [("t_ln1g", 16), ("t_ln1b", 16), ("t_ln2g", 16), ("t_ln2b", 16),
                      ("t_ubias", 16), ("t_mur", 16), ("t_muk", 16), ("t_muv", 16),
                      ("t_cmur", 16), ("t_cmuk", 16), ("t_gsc", 64), ("t_usc", 64),
                      ("t_dsc", 16), ("t_bd", 16), ("t_bproj", 16)]:
            setattr(self, nm, A(nm, [128, w], F32))
        self.t_blat = A("t_blat", [128, 1], F32)
        self.t_hmask = A("t_hmask", [128, 1], F32)
        self.t_ident = A("t_ident", [128, 128], BF16)
        self.t_tri = A("t_tri", [128, 2048], BF16)
        self.t_btab = A("t_btab", [128, 89], F32)
        self.t_consts = A("t_consts", [128, 4], F32)
        self.t_ones_col = A("t_ones_col", [128, 1], BF16)
        self.t_ones_row_f = A("t_ones_row_f", [1, 128], F32)
        self.t_n128_row = A("t_n128_row", [1, 128], BF16)
        self.t_rows = A("t_rows", [1, 3072], F32)
        P = nc.alloc_psum_tensor
        self.ps = [P(f"ps{i}", [128, 512], F32) for i in range(4)]
        self.ps_r0 = P("ps_r0", [1, 512], F32)
        self.ps_r1 = P("ps_r1", [1, 512], F32)
        self.ps_b0 = P("ps_b0", [128, 512], F32)
        self.ps_tb = P("ps_tb", [128, 1024], BF16)
        rem = nc.sbuf_bytes_remaining
        self.AR_B = rem - 1024
        self.arena = A("arena", [128, self.AR_B], U8)
        assert self.AR_B >= 189000, f"arena too small: {self.AR_B}"

    def carve(self, off, n, dt):
        esz = {F32: 4, BF16: 2, U8: 1}[dt]
        assert off + n * esz <= self.AR_B, (off, n, esz, self.AR_B)
        return self.arena[:, off:off + n * esz].bitcast(dt)

    def build(self):
        kb = self.kb
        self.sW = kb.sem("wdma")
        self.sX = kb.sem("xdma")
        self.sS = kb.sem("store")
        self.sP = kb.sem("pe")
        self.sV = kb.sem("dve")
        self.sA = kb.sem("act")
        self.sC = kb.sem("cc")
        self._bank_free = [None] * 4
        self._bank_i = 0
        self.ph0_init()
        self.ph1_ln1()
        if self.tap == "h1":
            self.tap_out_hbuf(); kb.finalize(); return
        self.ph2_tm()
        if self.tap == "tmout":
            self.tap_dram(self.tmout_d, WEXT); kb.finalize(); return
        self.ph3_ln2()
        if self.tap == "xb":
            self.tap_dram(self.xb_d, WEXT); kb.finalize(); return
        if self.tap == "h2":
            self.tap_out_hbuf(); kb.finalize(); return
        self.ph4_cm()
        if self.tap == "x1":
            self.tap_dram(self.x1_d, TOK); kb.finalize(); return
        self.ph5_ffn(self.x1_d, self.x2_d)
        if self.tap == "x2":
            self.tap_dram(self.x2_d, TOK); kb.finalize(); return
        self.ph6_attnb()
        if self.tap in ("qk", "yatt", "kfull"):
            kb.finalize(); return
        if self.tap == "x3":
            self.tap_dram(self.x3_d, TOK); kb.finalize(); return
        self.ph5_ffn(self.x3_d, self.out)
        kb.wait("sync", self.sS)
        kb.finalize()

    def tap_out_hbuf(self):
        kb = self.kb
        hb = self.carve(0, NCH * 2177, BF16)
        tbuf = self.carve(100 * 1024, 512, F32)
        kb.wait("vector", self.sA)
        for (c0, w) in EXT_TILES:
            for ci in range(NCH):
                src = hb[:, ci * 2177 + 1 + c0: ci * 2177 + 1 + c0 + w]
                nV = kb.op("vector", lambda e, s=src, d=tbuf[:, 0:w]: e.tensor_copy(d, s), self.sV)
                kb.wait("sync", self.sV, nV)
                nS = kb.dma("sync", self.tap_d[ci * 128:(ci + 1) * 128, c0:c0 + w],
                            tbuf[:, 0:w], self.sS)
                kb.wait("vector", self.sS, nS)
        kb.wait("sync", self.sS)

    def tap_dram(self, src_d, wtot):
        kb = self.kb
        tbuf = self.carve(100 * 1024, 512, F32)
        kb.wait("sync", self.sS)
        for ci in range(NCH):
            for c0 in range(0, wtot, 512):
                w = min(512, wtot - c0)
                nL = kb.dma("sync", tbuf[:, 0:w], src_d[ci * 128:(ci + 1) * 128, c0:c0 + w], self.sX)
                kb.wait("sync", self.sX, nL)
                nS = kb.dma("sync", self.tap_d[ci * 128:(ci + 1) * 128, c0:c0 + w], tbuf[:, 0:w], self.sS)
                kb.wait("sync", self.sS, nS)
        kb.wait("sync", self.sS)

    def ph0_init(self):
        kb = self.kb
        pairs = [(self.t_ln1g, self.v_ln1g), (self.t_ln1b, self.v_ln1b),
                 (self.t_ln2g, self.v_ln2g), (self.t_ln2b, self.v_ln2b),
                 (self.t_ubias, self.v_u), (self.t_mur, self.v_mur),
                 (self.t_muk, self.v_muk), (self.t_muv, self.v_muv),
                 (self.t_cmur, self.v_cmur), (self.t_cmuk, self.v_cmuk),
                 (self.t_gsc, self.v_gsc), (self.t_usc, self.v_usc),
                 (self.t_dsc, self.v_dsc), (self.t_blat, self.v_blat),
                 (self.t_bd, self.v_bd), (self.t_bproj, self.v_bproj),
                 (self.t_hmask, self.v_hmask), (self.t_ident, self.v_ident),
                 (self.t_tri, self.v_tri), (self.t_btab, self.v_btab)]
        for dst, src in pairs:
            kb.dma("sync", dst[:, :], src[:, :], self.sX)
        c = self.t_consts
        kb.op("vector", lambda e: e.memset(c[:, 0:1], 1.0 / 2048.0), self.sV)
        kb.op("vector", lambda e: e.memset(c[:, 1:2], EPS), self.sV)
        kb.op("vector", lambda e: e.memset(c[:, 2:3], LN1E8), self.sV)
        kb.op("vector", lambda e: e.memset(self.t_ones_col[:, :], 1.0), self.sV)
        kb.op("vector", lambda e: e.memset(self.t_ones_row_f[:, :], 1.0), self.sV)
        kb.op("vector", lambda e: e.memset(self.t_n128_row[:, :], -128.0), self.sV)
        kb.wait("vector", self.sX)
        kb.op("vector", lambda e: e.tensor_scalar_add(
            self.t_ubias[:, :], self.t_ubias[:, :], c[:, 2:3]), self.sV)

    def _load_xtile(self, dram, c0, w, off, sig):
        kb = self.kb
        dst = self.carve(off, NCH * 512, F32)
        for g in range(4):
            src = dram[g * 512:(g + 1) * 512, c0:c0 + w].rearrange("(a p) q -> p a q", p=128)
            d = dst[:, g * 2048:(g + 1) * 2048].rearrange("p (a q) -> p a q", a=4)[:, :, 0:w]
            kb.dma("sync", d, src, sig)
        return dst

    def _store_xtile(self, dram, c0, w, off, sig):
        kb = self.kb
        src = self.carve(off, NCH * 512, F32)
        for g in range(4):
            s = src[:, g * 2048:(g + 1) * 2048].rearrange("p (a q) -> p a q", a=4)[:, :, 0:w]
            d = dram[g * 512:(g + 1) * 512, c0:c0 + w].rearrange("(a p) q -> p a q", p=128)
            kb.dma("sync", d, s, sig)

    def _stats(self, get_chunk, w, off_sc, want_invA=False):
        kb, c = self.kb, self.t_consts
        sP, sV, sA = self.sP, self.sV, self.sA
        r = self.t_rows
        rows = dict(m=r[0:1, 0:w], e2=r[0:1, 512:512 + w], var=r[0:1, 1024:1024 + w],
                    inv=r[0:1, 1536:1536 + w], invc=r[0:1, 2048:2048 + w],
                    tmp=r[0:1, 2560:2560 + w])
        kb.wait("tensor", sV)
        kb.wait("scalar", sV)
        mm_n = [0, 0, 0, 0]
        for ci in range(NCH):
            sl = ci % 2
            xa = get_chunk(ci)
            xbf = self.carve(off_sc + sl * 2048, 512, BF16)[:, 0:w]
            xsq = self.carve(off_sc + 4096 + sl * 2048, 512, BF16)[:, 0:w]
            if mm_n[sl]:
                kb.wait("vector", sP, mm_n[sl])
            nV = kb.op("vector", lambda e, xa=xa, d=xbf: e.tensor_copy(d, xa), sV)
            if mm_n[2 + sl]:
                kb.wait("scalar", sP, mm_n[2 + sl])
            nA = kb.op("scalar", lambda e, xa=xa, d=xsq: e.activation(d, xa, AF.Square), sA)
            kb.wait("tensor", sV, nV)
            kb.wait("tensor", sA, nA)
            kb.op("tensor", lambda e, s=xbf, f=(ci == 0), l=(ci == NCH - 1):
                  e.matmul(self.ps_r0[:, 0:w], self.t_ones_col[:, :], s, start=f, stop=l), sP)
            mm_n[sl] = sP.n
            kb.op("tensor", lambda e, s=xsq, f=(ci == 0), l=(ci == NCH - 1):
                  e.matmul(self.ps_r1[:, 0:w], self.t_ones_col[:, :], s, start=f, stop=l), sP)
            mm_n[2 + sl] = sP.n
        kb.wait("vector", sP)
        kb.op("vector", lambda e: e.tensor_scalar_mul(rows["m"], self.ps_r0[0:1, 0:w], c[0:1, 0:1]), sV)
        kb.op("vector", lambda e: e.tensor_scalar_mul(rows["e2"], self.ps_r1[0:1, 0:w], c[0:1, 0:1]), sV)
        kb.op("vector", lambda e: e.tensor_mul(rows["tmp"], rows["m"], rows["m"]), sV)
        kb.op("vector", lambda e: e.tensor_sub(rows["var"], rows["e2"], rows["tmp"]), sV)
        kb.op("vector", lambda e: e.tensor_scalar_add(rows["tmp"], rows["var"], c[0:1, 1:2]), sV)
        nV = kb.op("vector", lambda e: e.reciprocal(rows["inv"], rows["tmp"]), sV)
        kb.wait("scalar", sV, nV)
        nA = kb.op("scalar", lambda e: e.activation(rows["inv"], rows["inv"], AF.Sqrt), sA)
        kb.wait("vector", sA, nA)
        if want_invA:
            kb.op("vector", lambda e: e.tensor_mul(rows["tmp"], rows["inv"], rows["inv"]), sV)
            kb.op("vector", lambda e: e.tensor_mul(rows["tmp"], rows["var"], rows["tmp"]), sV)
            kb.op("vector", lambda e: e.tensor_scalar_add(rows["tmp"], rows["tmp"], c[0:1, 1:2]), sV)
            nV = kb.op("vector", lambda e: e.reciprocal(rows["e2"], rows["tmp"]), sV)
            kb.wait("scalar", sV, nV)
            nA = kb.op("scalar", lambda e: e.activation(rows["e2"], rows["e2"], AF.Sqrt), sA)
            kb.wait("vector", sA, nA)
            kb.op("vector", lambda e: e.tensor_mul(rows["invc"], rows["inv"], rows["e2"]), sV)
        return rows

    def _bcast(self, row, w, off):
        kb = self.kb
        dst = self.carve(off, 512, F32)[:, 0:w]
        kb.wait("tensor", self.sV)
        kb.wait("tensor", self.sA)
        nP = kb.op("tensor", lambda e, r=row, w=w: e.matmul(
            self.ps_b0[:, 0:w], self.t_ones_row_f[:, :], r, start=True, stop=True), self.sP)
        kb.wait("vector", self.sP, nP)
        nV = kb.op("vector", lambda e, d=dst, w=w: e.tensor_copy(d, self.ps_b0[:, 0:w]), self.sV)
        return dst, nV

    def ph1_ln1(self):
        kb = self.kb
        hb = self.carve(0, NCH * 2177, BF16)
        O_XT, O_SC, O_MB, O_IB, O_TT = 69664, 102432, 110624, 112672, 114720
        for (c0, w) in EXT_TILES:
            kb.wait("sync", self.sV)
            kb.wait("sync", self.sA)
            kb.wait("sync", self.sS)
            xt = self._load_xtile(self.x_ext, c0, w, O_XT, self.sX)
            nL = self.sX.n
            kb.wait("vector", self.sX, nL)
            kb.wait("scalar", self.sX, nL)
            rows = self._stats(lambda ci: xt[:, ci * 512:ci * 512 + w], w, O_SC, want_invA=True)
            kb.wait("sync", self.sV)
            kb.wait("sync", self.sA)
            kb.dma("sync", self.rows_d[0:1, c0:c0 + w], rows["m"], self.sS)
            kb.dma("sync", self.rows_d[1:2, c0:c0 + w], rows["inv"], self.sS)
            m_sb, _ = self._bcast(rows["m"], w, O_MB)
            ic_sb, nV = self._bcast(rows["invc"], w, O_IB)
            kb.wait("vector", self.sV, nV)
            tts = [self.carve(O_TT, 512, F32)[:, 0:w],
                   self.carve(O_TT + 2048, 512, F32)[:, 0:w]]
            act_n = [0, 0]
            for ci in range(NCH):
                sl = ci % 2
                tt = tts[sl]
                xa = xt[:, ci * 512:ci * 512 + w]
                if act_n[sl]:
                    kb.wait("vector", self.sA, act_n[sl])
                kb.op("vector", lambda e, xa=xa, m=m_sb, d=tt: e.tensor_sub(d, xa, m), self.sV)
                nV = kb.op("vector", lambda e, d=tt, i=ic_sb: e.tensor_mul(d, d, i), self.sV)
                kb.wait("scalar", self.sV, nV)
                dst = hb[:, ci * 2177 + 1 + c0: ci * 2177 + 1 + c0 + w]
                act_n[sl] = kb.op("scalar", lambda e, s=tt, d=dst, ci=ci: e.activation(
                    d, s, AF.Identity, bias=self.t_ln1b[:, ci:ci + 1],
                    scale=self.t_ln1g[:, ci:ci + 1]), self.sA)
        kb.wait("vector", self.sA)
        for ci in range(NCH):
            kb.op("vector", lambda e, ci=ci: e.memset(hb[:, ci * 2177:ci * 2177 + 1], 0.0), self.sV)
            kb.op("vector", lambda e, ci=ci: e.tensor_scalar_mul(
                hb[:, ci * 2177 + 128:ci * 2177 + 129],
                hb[:, ci * 2177 + 128:ci * 2177 + 129], self.t_hmask[:, 0:1]), self.sV)

    def _mix(self, hb, mix, mu, h0, hw, o_dtmp):
        kb = self.kb
        dtmp = self.carve(o_dtmp, 1152, BF16)
        kb.wait("vector", self.sP)
        for ci in range(NCH):
            h = hb[:, ci * 2177 + 1 + h0: ci * 2177 + 1 + h0 + hw]
            hp = hb[:, ci * 2177 + h0: ci * 2177 + h0 + hw]
            d = dtmp[:, 0:hw]
            m = mix[:, ci * 1152:ci * 1152 + hw]
            kb.op("vector", lambda e, h=h, hp=hp, d=d: e.tensor_sub(d, h, hp), self.sV)
            kb.op("vector", lambda e, d=d, hp=hp, m=m, ci=ci, mu=mu: e.scalar_tensor_tensor(
                m, d, mu[:, ci:ci + 1], hp, ALU.mult, ALU.add), self.sV)
        kb.wait("tensor", self.sV)

    def ph2_tm(self):
        kb = self.kb
        hb = self.carve(0, NCH * 2177, BF16)
        O_MIX, O_RS = 69664, 106528
        O_W, O_SS, O_DT, O_OT = 143392, 151584, 153888, 156192
        kb.wait("tensor", self.sV)
        for hi, (h0, hw) in enumerate(HALF_RANGE):
            mix = self.carve(O_MIX, NCH * 1152, BF16)
            rsb = self.carve(O_RS, NCH * 1152, BF16)
            sstr = self.carve(O_SS, 1152, BF16)
            tiles = HALVES[hi]
            mixf = lambda a, ts0, tw: mix[:, a * 1152 + (ts0 - h0): a * 1152 + (ts0 - h0) + tw]
            rsf = lambda a, ts0, tw: rsb[:, a * 1152 + (ts0 - h0): a * 1152 + (ts0 - h0) + tw]
            self._mix(hb, mix, self.t_mur, h0, hw, O_DT)
            def evac_r(s, ts0, tw, bank):
                dst = rsb[:, s * 1152 + (ts0 - h0): s * 1152 + (ts0 - h0) + tw]
                nA = kb.op("scalar", lambda e, b=bank, d=dst, tw=tw: e.activation(
                    d, b[:, 0:tw], AF.Sigmoid), self.sA)
                return ("scalar", self.sA, nA)
            self.mm_sweep(self.w_tmr, mixf, tiles, evac_r, O_W)
            self._mix(hb, mix, self.t_muk, h0, hw, O_DT)
            def evac_k(s, ts0, tw, bank):
                dst = sstr[:, (ts0 - h0):(ts0 - h0) + tw]
                kb.wait("scalar", self.sV)
                nA = kb.op("scalar", lambda e, b=bank, d=dst, s=s, tw=tw: e.activation(
                    d, b[:, 0:tw], AF.Sigmoid, bias=self.t_ubias[:, s:s + 1]), self.sA)
                kb.wait("vector", self.sA, nA)
                r_ap = rsb[:, s * 1152 + (ts0 - h0): s * 1152 + (ts0 - h0) + tw]
                nV = kb.op("vector", lambda e, r=r_ap, d=dst: e.tensor_mul(r, d, r), self.sV)
                return ("vector", self.sV, nV)
            self.mm_sweep(self.w_tmk, mixf, tiles, evac_k, O_W)
            self._mix(hb, mix, self.t_muv, h0, hw, O_DT)
            def evac_v(s, ts0, tw, bank):
                r_ap = rsb[:, s * 1152 + (ts0 - h0): s * 1152 + (ts0 - h0) + tw]
                nV = kb.op("vector", lambda e, b=bank, r=r_ap, tw=tw: e.tensor_mul(
                    r, b[:, 0:tw], r), self.sV)
                return ("vector", self.sV, nV)
            self.mm_sweep(self.w_tmv, mixf, tiles, evac_v, O_W)
            ot = self.carve(O_OT, 1024, F32)
            def evac_o(s, ts0, tw, bank, slot=[0]):
                sl = slot[0]; slot[0] ^= 1
                dst = ot[:, sl * 512: sl * 512 + tw]
                kb.wait("vector", self.sS)
                nV = kb.op("vector", lambda e, b=bank, d=dst, tw=tw: e.tensor_copy(d, b[:, 0:tw]), self.sV)
                kb.wait("sync", self.sV, nV)
                kb.dma("sync", self.tmout_d[s * 128:(s + 1) * 128, ts0:ts0 + tw], dst, self.sS)
                return ("vector", self.sV, nV)
            self.mm_sweep(self.w_tmo, rsf, tiles, evac_o, O_W)

    def mm_sweep(self, wdram, rhs_fn, tiles, evac, o_w, nstrips=16, nacc=NCH, wbytes=4096):
        kb = self.kb
        kb.wait("gpsimd", self.sP)
        wsl = [self.carve(o_w + i * wbytes, wbytes // 2, BF16) for i in range(2)]
        wfree = [None, None]
        for s in range(nstrips):
            sl = s % 2
            if wfree[sl] is not None:
                kb.wait("gpsimd", wfree[sl][0], wfree[sl][1])
            nW = kb.dma("gpsimd", wsl[sl][:, :], wdram[s].rearrange("p a q -> p (a q)"), self.sW)
            kb.wait("tensor", self.sW, nW)
            for (ts0, tw) in tiles:
                bi = self._bank_i
                bank = self.ps[bi]
                if self._bank_free[bi] is not None:
                    _, sem, n = self._bank_free[bi]
                    kb.wait("tensor", sem, n)
                for a in range(nacc):
                    kb.op("tensor", lambda e, l=wsl[sl][:, a * 128:(a + 1) * 128],
                          r=rhs_fn(a, ts0, tw), b=bank, tw=tw,
                          f=(a == 0), la=(a == nacc - 1): e.matmul(
                              b[:, 0:tw], l, r, start=f, stop=la),
                          self.sP if a == nacc - 1 else None)
                nP = self.sP.n
                kb.wait("vector", self.sP, nP)
                kb.wait("scalar", self.sP, nP)
                self._bank_free[bi] = evac(s, ts0, tw, bank)
                self._bank_i = (bi + 1) % 4
            wfree[sl] = (self.sP, self.sP.n)

    def ph3_ln2(self):
        kb = self.kb
        hb = self.carve(0, NCH * 2177, BF16)
        O_XT, O_TM = 69664, 102432
        O_RW, O_MB, O_IB, O_SC, O_TT = 135200, 139296, 141344, 143392, 151584
        for (c0, w) in EXT_TILES:
            kb.wait("sync", self.sV)
            kb.wait("sync", self.sA)
            kb.wait("sync", self.sS)
            xt = self._load_xtile(self.x_ext, c0, w, O_XT, self.sX)
            tm = self._load_xtile(self.tmout_d, c0, w, O_TM, self.sX)
            rowbuf = self.carve(O_RW, 1024, F32)
            kb.dma("sync", rowbuf[0:1, 0:w], self.rows_d[0:1, c0:c0 + w], self.sX)
            kb.dma("sync", rowbuf[0:1, 512:512 + w], self.rows_d[1:2, c0:c0 + w], self.sX)
            nL = self.sX.n
            kb.wait("vector", self.sX, nL)
            kb.wait("scalar", self.sX, nL)
            kb.wait("tensor", self.sX, nL)
            m_sb, _ = self._bcast(rowbuf[0:1, 0:w], w, O_MB)
            i_sb, nV = self._bcast(rowbuf[0:1, 512:512 + w], w, O_IB)
            kb.wait("vector", self.sV, nV)
            tt = self.carve(O_TT, 512, F32)[:, 0:w]
            for ci in range(NCH):
                xa = xt[:, ci * 512:ci * 512 + w]
                ta = tm[:, ci * 512:ci * 512 + w]
                kb.op("vector", lambda e, xa=xa, m=m_sb, d=tt: e.tensor_sub(d, xa, m), self.sV)
                kb.op("vector", lambda e, d=tt, i=i_sb: e.tensor_mul(d, d, i), self.sV)
                kb.op("vector", lambda e, d=ta, a=tt: e.tensor_add(d, a, d), self.sV)
                nV = kb.op("vector", lambda e, d=xa, b=ta: e.tensor_add(d, b, d), self.sV)
            kb.wait("sync", self.sV, nV)
            self._store_xtile(self.xb_d, c0, w, O_XT, self.sS)
            rows = self._stats(lambda ci: tm[:, ci * 512:ci * 512 + w], w, O_SC)
            m2, _ = self._bcast(rows["m"], w, O_MB)
            i2, nV = self._bcast(rows["inv"], w, O_IB)
            kb.wait("vector", self.sV, nV)
            tts2 = [self.carve(O_TT, 512, F32)[:, 0:w],
                    self.carve(O_TT + 2048, 512, F32)[:, 0:w]]
            act_n = [0, 0]
            for ci in range(NCH):
                sl = ci % 2
                tt2 = tts2[sl]
                ta = tm[:, ci * 512:ci * 512 + w]
                if act_n[sl]:
                    kb.wait("vector", self.sA, act_n[sl])
                kb.op("vector", lambda e, xa=ta, m=m2, d=tt2: e.tensor_sub(d, xa, m), self.sV)
                nV = kb.op("vector", lambda e, d=tt2, i=i2: e.tensor_mul(d, d, i), self.sV)
                kb.wait("scalar", self.sV, nV)
                dst = hb[:, ci * 2177 + 1 + c0: ci * 2177 + 1 + c0 + w]
                act_n[sl] = kb.op("scalar", lambda e, s=tt2, d=dst, ci=ci: e.activation(
                    d, s, AF.Identity, bias=self.t_ln2b[:, ci:ci + 1],
                    scale=self.t_ln2g[:, ci:ci + 1]), self.sA)
        kb.wait("vector", self.sA)
        for ci in range(NCH):
            kb.op("vector", lambda e, ci=ci: e.memset(hb[:, ci * 2177:ci * 2177 + 1], 0.0), self.sV)
            kb.op("vector", lambda e, ci=ci: e.tensor_scalar_mul(
                hb[:, ci * 2177 + 128:ci * 2177 + 129],
                hb[:, ci * 2177 + 128:ci * 2177 + 129], self.t_hmask[:, 0:1]), self.sV)

    def ph4_cm(self):
        kb = self.kb
        hb = self.carve(0, NCH * 2177, BF16)
        O_MIX, O_K2 = 69664, 106528
        O_W, O_R2, O_TK, O_XB = 143392, 159776, 162080, 163104
        kb.wait("tensor", self.sV)
        kb.wait("sync", self.sS)
        for hi, (h0, hw) in enumerate(HALF_RANGE):
            mix = self.carve(O_MIX, NCH * 1152, BF16)
            k2b = self.carve(O_K2, NCH * 1152, BF16)
            tiles = HALVES[hi]
            mixf = lambda a, ts0, tw: mix[:, a * 1152 + (ts0 - h0): a * 1152 + (ts0 - h0) + tw]
            k2f = lambda a, ts0, tw: k2b[:, a * 1152 + (ts0 - h0): a * 1152 + (ts0 - h0) + tw]
            self._mix(hb, mix, self.t_cmuk, h0, hw, 171296)
            tk = self.carve(O_TK, 512, BF16)
            def evac_k2(s, ts0, tw, bank):
                kb.wait("vector", self.sA)
                nV = kb.op("vector", lambda e, b=bank, d=tk[:, 0:tw], tw=tw:
                           e.tensor_scalar_max(d, b[:, 0:tw], 0.0), self.sV)
                kb.wait("scalar", self.sV, nV)
                dst = k2b[:, s * 1152 + (ts0 - h0): s * 1152 + (ts0 - h0) + tw]
                nA = kb.op("scalar", lambda e, s_=tk[:, 0:tw], d=dst: e.activation(
                    d, s_, AF.Square), self.sA)
                return ("scalar", self.sA, nA)
            self.mm_sweep(self.w_cmk, mixf, tiles, evac_k2, O_W)
            self._mix(hb, mix, self.t_cmur, h0, hw, 171296)
            r2t = self.carve(O_R2, 1152, BF16)
            xbsl = self.carve(O_XB, 2048, F32)
            kb.wait("gpsimd", self.sP)
            wR = [self.carve(O_W + i * 4096, 2048, BF16) for i in range(2)]
            wV = [self.carve(O_W + 8192 + i * 4096, 2048, BF16) for i in range(2)]
            wfree = [None, None]
            for s in range(16):
                sl = s % 2
                if wfree[sl] is not None:
                    kb.wait("gpsimd", wfree[sl][0], wfree[sl][1])
                kb.dma("gpsimd", wR[sl][:, :], self.w_cmr[s].rearrange("p a q -> p (a q)"), self.sW)
                nW = kb.dma("gpsimd", wV[sl][:, :], self.w_cmv[s].rearrange("p a q -> p (a q)"), self.sW)
                kb.wait("tensor", self.sW, nW)
                for (ts0, tw) in tiles:
                    bi = self._bank_i
                    bank = self.ps[bi]
                    if self._bank_free[bi] is not None:
                        _, sem, n = self._bank_free[bi]
                        kb.wait("tensor", sem, n)
                    for a in range(NCH):
                        kb.op("tensor", lambda e, l=wR[sl][:, a * 128:(a + 1) * 128],
                              r=mixf(a, ts0, tw), b=bank, tw=tw, f=(a == 0), la=(a == NCH - 1):
                              e.matmul(b[:, 0:tw], l, r, start=f, stop=la),
                              self.sP if a == NCH - 1 else None)
                    nPr = self.sP.n
                    kb.wait("scalar", self.sP, nPr)
                    kb.wait("scalar", self.sV)
                    dR = r2t[:, (ts0 - h0):(ts0 - h0) + tw]
                    nA = kb.op("scalar", lambda e, b=bank, d=dR, tw=tw: e.activation(
                        d, b[:, 0:tw], AF.Sigmoid), self.sA)
                    self._bank_free[bi] = ("scalar", self.sA, nA)
                    self._bank_i = (bi + 1) % 4
                    bi = self._bank_i
                    bank2 = self.ps[bi]
                    if self._bank_free[bi] is not None:
                        _, sem, n = self._bank_free[bi]
                        kb.wait("tensor", sem, n)
                    kb.wait("sync", self.sV)
                    xsl = xbsl[:, 0:512][:, 0:tw]
                    nX = kb.dma("sync", xsl, self.xb_d[s * 128:(s + 1) * 128, ts0:ts0 + tw], self.sX)
                    for a in range(NCH):
                        kb.op("tensor", lambda e, l=wV[sl][:, a * 128:(a + 1) * 128],
                              r=k2f(a, ts0, tw), b=bank2, tw=tw, f=(a == 0), la=(a == NCH - 1):
                              e.matmul(b[:, 0:tw], l, r, start=f, stop=la),
                              self.sP if a == NCH - 1 else None)
                    nP = self.sP.n
                    kb.wait("vector", self.sP, nP)
                    kb.wait("vector", self.sA, nA)
                    kb.wait("vector", self.sX, nX)
                    kb.wait("vector", self.sS)
                    ot = xbsl[:, 1024:1536][:, 0:tw]
                    kb.op("vector", lambda e, b=bank2, r=dR, d=ot, tw=tw: e.tensor_mul(
                        d, b[:, 0:tw], r), self.sV)
                    nV = kb.op("vector", lambda e, d=ot, x=xsl: e.tensor_add(d, d, x), self.sV)
                    self._bank_free[bi] = ("vector", self.sV, nV)
                    self._bank_i = (bi + 1) % 4
                    kb.wait("sync", self.sV, nV)
                    o0 = max(ts0, HALO)
                    if o0 < ts0 + tw:
                        off = o0 - ts0
                        kb.dma("sync", self.x1_d[s * 128:(s + 1) * 128, o0 - HALO:ts0 + tw - HALO],
                               ot[:, off:tw], self.sS)
                wfree[sl] = (self.sP, self.sP.n)

    def ph5_ffn(self, src_d, dst_d):
        kb = self.kb
        mid = self.carve(0, 64 * 512, BF16)
        h3 = self.carve(65536, NCH * 512, BF16)
        rt = self.carve(81920, 1024, BF16)
        sgt = self.carve(83968, 1024, BF16)
        ot = self.carve(86016, 1024, F32)
        O_XT = 90112
        O_GU, O_DN = 122880, 139264
        O_MB, O_IB, O_TT, O_SC = 172032, 174080, 176128, 178176
        kb.wait("gpsimd", self.sP)
        wg = [self.carve(O_GU + i * 4096, 2048, BF16) for i in range(2)]
        wu = [self.carve(O_GU + 8192 + i * 4096, 2048, BF16) for i in range(2)]
        wdn = [self.carve(O_DN + i * 16384, 8192, BF16) for i in range(2)]
        wfree_g = [None, None]; wfree_u = [None, None]; wfree_d = [None, None]
        sg_free = [0, 0]; od_free = [0, 0]
        for (c0, w) in OWN_TILES:
            kb.wait("sync", self.sV)
            kb.wait("sync", self.sA)
            kb.wait("sync", self.sS)
            xt = self._load_xtile(src_d, c0, w, O_XT, self.sX)
            nL = self.sX.n
            kb.wait("vector", self.sX, nL)
            kb.wait("scalar", self.sX, nL)
            rows = self._stats(lambda ci: xt[:, ci * 512:ci * 512 + w], w, O_SC)
            m_sb, _ = self._bcast(rows["m"], w, O_MB)
            i_sb, nV = self._bcast(rows["inv"], w, O_IB)
            kb.wait("vector", self.sV, nV)
            tt = self.carve(O_TT, 512, F32)[:, 0:w]
            for ci in range(NCH):
                xa = xt[:, ci * 512:ci * 512 + w]
                kb.op("vector", lambda e, xa=xa, m=m_sb, d=tt: e.tensor_sub(d, xa, m), self.sV)
                kb.op("vector", lambda e, d=h3[:, ci * 512:ci * 512 + w], s=tt, i=i_sb:
                      e.tensor_mul(d, s, i), self.sV)
            nVh = self.sV.n
            kb.wait("tensor", self.sV, nVh)
            for a in range(NCH):
                kb.op("tensor", lambda e, s=h3[:, a * 512:a * 512 + w], f=(a == 0), la=(a == NCH - 1):
                      e.matmul(self.ps_r0[:, 0:w], self.t_ones_col[:, :], s, start=f, stop=la),
                      self.sP if a == NCH - 1 else None)
            kb.wait("vector", self.sP)
            nV = kb.op("vector", lambda e: e.tensor_copy(rt[0:1, 0:w], self.ps_r0[0:1, 0:w]), self.sV)
            kb.wait("tensor", self.sV, nV)
            for ic in range(64):
                sl = ic % 2
                if wfree_g[sl] is not None:
                    kb.wait("gpsimd", wfree_g[sl][0], wfree_g[sl][1])
                kb.dma("gpsimd", wg[sl][:, :], self.w_gate[ic].rearrange("p a q -> p (a q)"), self.sW)
                if wfree_u[sl] is not None:
                    kb.wait("gpsimd", wfree_u[sl][0], wfree_u[sl][1])
                nWu = kb.dma("gpsimd", wu[sl][:, :], self.w_up[ic].rearrange("p a q -> p (a q)"), self.sW)
                kb.wait("tensor", self.sW, nWu)
                bi = self._bank_i
                bank_g = self.ps[bi]
                if self._bank_free[bi] is not None:
                    _, sem, n = self._bank_free[bi]
                    kb.wait("tensor", sem, n)
                for a in range(NCH):
                    kb.op("tensor", lambda e, l=wg[sl][:, a * 128:(a + 1) * 128],
                          r=h3[:, a * 512:a * 512 + w], b=bank_g, w=w, f=(a == 0):
                          e.matmul(b[:, 0:w], l, r, start=f, stop=False), None)
                kb.op("tensor", lambda e, b=bank_g, w=w: e.matmul(
                    b[:, 0:w], self.t_n128_row[:, :], rt[0:1, 0:w], start=False, stop=True), self.sP)
                nPg = self.sP.n
                big = bi
                self._bank_i = (bi + 1) % 4
                bi = self._bank_i
                bank_u = self.ps[bi]
                if self._bank_free[bi] is not None:
                    _, sem, n = self._bank_free[bi]
                    kb.wait("tensor", sem, n)
                for a in range(NCH):
                    kb.op("tensor", lambda e, l=wu[sl][:, a * 128:(a + 1) * 128],
                          r=h3[:, a * 512:a * 512 + w], b=bank_u, w=w, f=(a == 0):
                          e.matmul(b[:, 0:w], l, r, start=f, stop=False), None)
                kb.op("tensor", lambda e, b=bank_u, w=w: e.matmul(
                    b[:, 0:w], self.t_n128_row[:, :], rt[0:1, 0:w], start=False, stop=True), self.sP)
                nPu = self.sP.n
                kb.wait("scalar", self.sP, nPg)
                if sg_free[sl]:
                    kb.wait("scalar", self.sV, sg_free[sl])
                sg = sgt[:, sl * 512:sl * 512 + w]
                nA = kb.op("scalar", lambda e, b=bank_g, d=sg, ic=ic, w=w: e.activation(
                    d, b[:, 0:w], AF.Silu, scale=self.t_gsc[:, ic:ic + 1]), self.sA)
                self._bank_free[big] = ("scalar", self.sA, nA)
                kb.wait("vector", self.sP, nPu)
                kb.wait("vector", self.sA, nA)
                nV = kb.op("vector", lambda e, b=bank_u, s=sg, d=mid[:, ic * 512:ic * 512 + w],
                           ic=ic, w=w: e.scalar_tensor_tensor(
                               d, b[:, 0:w], self.t_usc[:, ic:ic + 1], s, ALU.mult, ALU.mult), self.sV)
                sg_free[sl] = nV
                self._bank_free[bi] = ("vector", self.sV, nV)
                self._bank_i = (bi + 1) % 4
                wfree_g[sl] = (self.sP, nPg)
                wfree_u[sl] = (self.sP, nPu)
            nVm = self.sV.n
            kb.wait("tensor", self.sV, nVm)
            for a in range(64):
                kb.op("tensor", lambda e, s=mid[:, a * 512:a * 512 + w], f=(a == 0), la=(a == 63):
                      e.matmul(self.ps_r0[:, 0:w], self.t_ones_col[:, :], s, start=f, stop=la),
                      self.sP if a == 63 else None)
            kb.wait("vector", self.sP)
            nV = kb.op("vector", lambda e: e.tensor_copy(rt[0:1, 512:512 + w], self.ps_r0[0:1, 0:w]), self.sV)
            kb.wait("tensor", self.sV, nV)
            for s in range(16):
                sl = s % 2
                if wfree_d[sl] is not None:
                    kb.wait("gpsimd", wfree_d[sl][0], wfree_d[sl][1])
                nWd = kb.dma("gpsimd", wdn[sl][:, :], self.w_down[s].rearrange("p a q -> p (a q)"), self.sW)
                kb.wait("tensor", self.sW, nWd)
                bi = self._bank_i
                bank = self.ps[bi]
                if self._bank_free[bi] is not None:
                    _, sem, n = self._bank_free[bi]
                    kb.wait("tensor", sem, n)
                for a in range(64):
                    kb.op("tensor", lambda e, l=wdn[sl][:, a * 128:(a + 1) * 128],
                          r=mid[:, a * 512:a * 512 + w], b=bank, w=w, f=(a == 0):
                          e.matmul(b[:, 0:w], l, r, start=f, stop=False), None)
                kb.op("tensor", lambda e, b=bank, w=w: e.matmul(
                    b[:, 0:w], self.t_n128_row[:, :], rt[0:1, 512:512 + w], start=False, stop=True), self.sP)
                nP = self.sP.n
                kb.wait("vector", self.sP, nP)
                if od_free[sl]:
                    kb.wait("vector", self.sS, od_free[sl])
                od = ot[:, sl * 512:sl * 512 + w]
                nV = kb.op("vector", lambda e, b=bank, d=od, s=s, w=w,
                           x=xt[:, s * 512:s * 512 + w]: e.scalar_tensor_tensor(
                               d, b[:, 0:w], self.t_dsc[:, s:s + 1], x, ALU.mult, ALU.add), self.sV)
                self._bank_free[bi] = ("vector", self.sV, nV)
                self._bank_i = (bi + 1) % 4
                kb.wait("sync", self.sV, nV)
                od_free[sl] = kb.dma("sync", dst_d[s * 128:(s + 1) * 128, c0:c0 + w], od, self.sS)
                wfree_d[sl] = (self.sP, nP)

    def ph6_attnb(self):
        kb = self.kb
        h3f = self.carve(0, NCH * 2048, BF16)
        qb = self.carve(65536, NCH * 2048, BF16)
        O_XT = 131072
        O_WSL = 163840
        k_own = self.carve(172032, 2048, BF16)
        O_MB, O_IB = 176128, 178176
        O_SC = 176128
        O_TT = 184320
        kb.wait("gpsimd", self.sP)
        for (c0, w) in OWN_TILES:
            kb.wait("sync", self.sV)
            kb.wait("sync", self.sA)
            kb.wait("sync", self.sS)
            xt = self._load_xtile(self.x2_d, c0, w, O_XT, self.sX)
            nL = self.sX.n
            kb.wait("vector", self.sX, nL)
            kb.wait("scalar", self.sX, nL)
            rows = self._stats(lambda ci: xt[:, ci * 512:ci * 512 + w], w, O_SC)
            m_sb, _ = self._bcast(rows["m"], w, O_MB)
            i_sb, nV = self._bcast(rows["inv"], w, O_IB)
            kb.wait("vector", self.sV, nV)
            tt = self.carve(O_TT, 512, F32)[:, 0:w]
            for ci in range(NCH):
                xa = xt[:, ci * 512:ci * 512 + w]
                kb.op("vector", lambda e, xa=xa, m=m_sb, d=tt: e.tensor_sub(d, xa, m), self.sV)
                kb.op("vector", lambda e, d=h3f[:, ci * 2048 + c0:ci * 2048 + c0 + w], s=tt, i=i_sb:
                      e.tensor_mul(d, s, i), self.sV)
        nVh = self.sV.n
        kb.wait("tensor", self.sV, nVh)
        h3ff = lambda a, c0, w: h3f[:, a * 2048 + c0:a * 2048 + c0 + w]
        def evac_q(s, c0, w, bank):
            nA = kb.op("scalar", lambda e, b=bank, d=qb[:, s * 2048 + c0:s * 2048 + c0 + w],
                       s_=s, w=w: e.activation(d, b[:, 0:w], AF.Identity,
                                               bias=self.t_bd[:, s_:s_ + 1]), self.sA)
            return ("scalar", self.sA, nA)
        self.mm_sweep(self.w_d, h3ff, OWN_TILES, evac_q, O_WSL)
        def evac_klat(s, c0, w, bank):
            nA = kb.op("scalar", lambda e, b=bank, d=k_own[:, c0:c0 + w], w=w: e.activation(
                d, b[:, 0:w], AF.Identity, bias=self.t_blat[:, 0:1]), self.sA)
            return ("scalar", self.sA, nA)
        self.mm_sweep(self.w_lat, h3ff, OWN_TILES, evac_klat, O_WSL, nstrips=1)
        if self.tap == "qk":
            self._tap_qk(qb, k_own)
            return
        kb.wait("sync", self.sA)
        nS = kb.dma("sync", self.kcc_in[:, :], k_own[:, :], self.sS)
        kb.wait("gpsimd", self.sS, nS)
        nC = kb.op("gpsimd", lambda e: e.collective_compute(
            "AllGather", ALU.bypass, replica_groups=[[0, 1], [2, 3], [4, 5], [6, 7]],
            ins=[self.kcc_in[:, :]], outs=[self.kcc_out[:, :]]), self.sC)
        kb.wait("sync", self.sC, nC)
        k_full = self.carve(O_XT, 4096, BF16)
        k_tm = self.carve(O_XT + 8192, 4096, BF16)
        k_tm_own = self.carve(O_XT + 16384, 2048, BF16)
        p_strip = self.carve(O_XT + 20480, 16 * 512, BF16)
        rb_sb = self.carve(167936, 512, F32)
        sums_row = self.carve(169984, 512, F32)
        kb.dma("sync", k_full[:, 0:2048], self.kcc_out[0:128, :], self.sX)
        nX = kb.dma("sync", k_full[:, 2048:4096], self.kcc_out[128:256, :], self.sX)
        kb.wait("tensor", self.sX, nX)
        def transpose_batch(src, n_tiles, dst):
            prev = 0
            for g in range(0, n_tiles, 4):
                nn = min(g + 4, n_tiles) - g
                if prev:
                    kb.wait("tensor", self.sV, prev)
                for j in range(g, g + nn):
                    kb.op("tensor", lambda e, s=src[:, j * 128:(j + 1) * 128],
                          d=self.ps_tb[:, (j % 4) * 128:(j % 4 + 1) * 128]:
                          e.transpose(d, s, self.t_ident[:, :]),
                          self.sP if j == g + nn - 1 else None)
                nP = self.sP.n
                kb.wait("vector", self.sP, nP)
                prev = kb.op("vector", lambda e, g=g, nn=nn, d=dst: e.tensor_copy(
                    d[:, g * 128:(g + nn) * 128], self.ps_tb[:, 0:nn * 128]), self.sV)
            kb.wait("tensor", self.sV)
        transpose_batch(k_full, 32, k_tm)
        transpose_batch(k_own, 16, k_tm_own)
        if self.tap == "kfull":
            kb.wait("vector", self.sV)
            tb3 = self.carve(184320, 1024, F32)
            for (c0, w) in [(0, 1024), (1024, 1024), (2048, 1024), (3072, 1024)]:
                nV = kb.op("vector", lambda e, c0=c0, w=w: e.tensor_copy(
                    tb3[:, 0:w], k_full[:, c0:c0 + w]), self.sV)
                kb.wait("sync", self.sV, nV)
                nS = kb.dma("sync", self.tap_d[0:128, c0:c0 + w], tb3[:, 0:w], self.sS)
                kb.wait("vector", self.sS, nS)
            for (c0, w) in [(0, 1024), (1024, 1024), (2048, 1024), (3072, 1024)]:
                nV = kb.op("vector", lambda e, c0=c0, w=w: e.tensor_copy(
                    tb3[:, 0:w], k_tm[:, c0:c0 + w]), self.sV)
                kb.wait("sync", self.sV, nV)
                nS = kb.dma("sync", self.tap_d[128:256, c0:c0 + w], tb3[:, 0:w], self.sS)
                kb.wait("vector", self.sS, nS)
            kb.wait("sync", self.sS)
            return
        import os
        lim = os.environ.get("DEV6D", "")
        nsl, nhd = (4, 16) if not lim else tuple(int(v) for v in lim.split(","))
        for si in range(nsl):
            Tn = TS_ATT[si]
            nint = Tn - 4
            for head in range(nhd):
                kb.wait("tensor", self.sV)
                q_ap = qb[:, head * 2048 + si * 512: head * 2048 + (si + 1) * 512]
                y_bank = self.ps[2]
                if self._bank_free[2] is not None:
                    _, sem, n = self._bank_free[2]
                    kb.wait("tensor", sem, n)
                first_y = True
                first_sum = True
                rounds = [list(range(Tn))[i:i + 16] for i in range(0, Tn, 16)]
                for rnd in rounds:
                    pmask_n = 0
                    for idx, js in enumerate(rnd):
                        diag = js >= nint
                        jd = js - nint
                        bsl = idx % 2
                        bnk = self.ps[bsl]
                        if self._bank_free[bsl] is not None:
                            _, sem, n = self._bank_free[bsl]
                            kb.wait("tensor", sem, n)
                        if diag:
                            lhsT = k_own[:, si * 512 + jd * 128: si * 512 + (jd + 1) * 128]
                            bias = self.t_btab[:, 88:89]
                        else:
                            lhsT = k_full[:, js * 128:(js + 1) * 128]
                            bias = self.t_btab[:, COLOFF[si] + js:COLOFF[si] + js + 1]
                        nP = kb.op("tensor", lambda e, l=lhsT, r=q_ap, b=bnk: e.matmul(
                            b[:, :], l, r, start=True, stop=True), self.sP)
                        kb.wait("scalar", self.sP, nP)
                        pdst = p_strip[:, idx * 512:(idx + 1) * 512]
                        nA = kb.op("scalar", lambda e, b=bnk, d=pdst, bias=bias: e.activation(
                            d, b[:, :], AF.Exp, bias=bias, scale=ESCALE), self.sA)
                        self._bank_free[bsl] = ("scalar", self.sA, nA)
                        if diag:
                            kb.wait("vector", self.sA, nA)
                            pmask_n = kb.op("vector", lambda e, d=pdst, jd=jd: e.tensor_mul(
                                d, d, self.t_tri[:, jd * 512:(jd + 1) * 512]), self.sV)
                    kb.wait("tensor", self.sA)
                    if pmask_n:
                        kb.wait("tensor", self.sV, pmask_n)
                    for idx, js in enumerate(rnd):
                        kb.op("tensor", lambda e, p=p_strip[:, idx * 512:(idx + 1) * 512],
                              f=(first_sum and idx == 0), la=(js == Tn - 1):
                              e.matmul(self.ps_r0[:, :], self.t_ones_col[:, :], p, start=f, stop=la),
                              self.sP if js == Tn - 1 else None)
                    first_sum = False
                    for idx, js in enumerate(rnd):
                        diag = js >= nint
                        jd = js - nint
                        lhsT = (k_tm_own[:, (si * 4 + jd) * 128:(si * 4 + jd + 1) * 128] if diag
                                else k_tm[:, js * 128:(js + 1) * 128])
                        kb.op("tensor", lambda e, l=lhsT, p=p_strip[:, idx * 512:(idx + 1) * 512],
                              f=(first_y and idx == 0), la=(js == Tn - 1):
                              e.matmul(y_bank[:, :], l, p, start=f, stop=la),
                              self.sP if js == Tn - 1 else None)
                    first_y = False
                nP = self.sP.n
                kb.wait("vector", self.sP, nP)
                kb.op("vector", lambda e: e.tensor_copy(sums_row[0:1, 0:512], self.ps_r0[0:1, :]), self.sV)
                nV = kb.op("vector", lambda e: e.reciprocal(
                    sums_row[0:1, 0:512], sums_row[0:1, 0:512]), self.sV)
                kb.wait("tensor", self.sV, nV)
                nP2 = kb.op("tensor", lambda e: e.matmul(
                    self.ps_b0[:, :], self.t_ones_row_f[:, :], sums_row[0:1, 0:512],
                    start=True, stop=True), self.sP)
                kb.wait("vector", self.sP, nP2)
                kb.op("vector", lambda e: e.tensor_copy(rb_sb[:, :], self.ps_b0[:, :]), self.sV)
                ydst = h3f[:, head * 2048 + si * 512: head * 2048 + (si + 1) * 512]
                nV = kb.op("vector", lambda e, d=ydst: e.tensor_mul(d, y_bank[:, :], rb_sb[:, :]), self.sV)
                self._bank_free[2] = ("vector", self.sV, nV)
        if self.tap == "yatt":
            kb.wait("vector", self.sV)
            tb2 = self.carve(184320, 512, F32)
            for s in range(16):
                for (c0, w) in OWN_TILES:
                    nV = kb.op("vector", lambda e, s_=s, c0=c0, w=w: e.tensor_copy(
                        tb2[:, 0:w], h3f[:, s_ * 2048 + c0:s_ * 2048 + c0 + w]), self.sV)
                    kb.wait("sync", self.sV, nV)
                    nS = kb.dma("sync", self.tap_d[s * 128:(s + 1) * 128, c0:c0 + w],
                                tb2[:, 0:w], self.sS)
                    kb.wait("vector", self.sS, nS)
            kb.wait("sync", self.sS)
            return
        nVy = self.sV.n
        kb.wait("tensor", self.sV, nVy)
        kb.wait("gpsimd", self.sP)
        x2sl = self.carve(65536, 1024, F32)
        oproj = self.carve(69632, 1024, F32)
        def evac_proj(s, c0, w, bank, st={"ti": 0}):
            ti = st["ti"]; st["ti"] = (ti + 1) % 2
            kb.wait("sync", self.sV)
            xs = x2sl[:, ti * 512:ti * 512 + w]
            nX = kb.dma("sync", xs, self.x2_d[s * 128:(s + 1) * 128, c0:c0 + w], self.sX)
            kb.wait("vector", self.sX, nX)
            kb.wait("vector", self.sS)
            od = oproj[:, ti * 512:ti * 512 + w]
            nV = kb.op("vector", lambda e, b=bank, d=od, s_=s, x=xs, w=w: e.scalar_tensor_tensor(
                d, b[:, 0:w], self.t_bproj[:, s_:s_ + 1], x, ALU.add, ALU.add), self.sV)
            kb.wait("sync", self.sV, nV)
            kb.dma("sync", self.x3_d[s * 128:(s + 1) * 128, c0:c0 + w], od, self.sS)
            return ("vector", self.sV, nV)
        yf = lambda a, c0, w: h3f[:, a * 2048 + c0:a * 2048 + c0 + w]
        self.mm_sweep(self.w_proj, yf, OWN_TILES, evac_proj, O_WSL)

    def _tap_qk(self, qb, k_own):
        kb = self.kb
        kb.wait("vector", self.sA)
        tb = self.carve(184320, 512, F32)
        for s in range(16):
            for (c0, w) in OWN_TILES:
                nV = kb.op("vector", lambda e, s_=s, c0=c0, w=w: e.tensor_copy(
                    tb[:, 0:w], qb[:, s_ * 2048 + c0:s_ * 2048 + c0 + w]), self.sV)
                kb.wait("sync", self.sV, nV)
                nS = kb.dma("sync", self.tap_d[s * 128:(s + 1) * 128, c0:c0 + w], tb[:, 0:w], self.sS)
                kb.wait("vector", self.sS, nS)
        for (c0, w) in OWN_TILES:
            nV = kb.op("vector", lambda e, c0=c0, w=w: e.tensor_copy(
                tb[:, 0:w], k_own[:, c0:c0 + w]), self.sV)
            kb.wait("sync", self.sV, nV)
            nS = kb.dma("sync", self.tap_d[2048:2176, c0:c0 + w], tb[:, 0:w], self.sS)
            kb.wait("vector", self.sS, nS)
        kb.wait("sync", self.sS)



def _build(tap=None):
    nc = bass.Bass("TRN2", target_bir_lowering=False)
    core = Core(nc, tap=tap)
    core.build()
    return nc


def _stage_inputs(inputs):
    g = inputs
    x = np.asarray(g["L0_attn_a_x"], dtype=np.float32)
    shared = dict(
        w_tmr=_stageT(np.asarray(g["L0_attn_a_tm_Wr"], np.float32)),
        w_tmk=_stageT(np.asarray(g["L0_attn_a_tm_Wk"], np.float32)),
        w_tmv=_stageT(np.asarray(g["L0_attn_a_tm_Wv"], np.float32)),
        w_tmo=_stageT(np.asarray(g["L0_attn_a_tm_Wout"], np.float32)),
        w_cmr=_stageT(np.asarray(g["L0_attn_a_cm_Wr"], np.float32)),
        w_cmk=_stageT(np.asarray(g["L0_attn_a_cm_Wk"], np.float32)),
        w_cmv=_stageT(np.asarray(g["L0_attn_a_cm_Wv"], np.float32)),
        w_lat=_stageN(np.asarray(g["L0_attn_b_W_lat"], np.float32)),
        w_d=_stageN(np.asarray(g["L0_attn_b_W_d"], np.float32)),
        w_proj=_stageN(np.asarray(g["L0_attn_b_W_proj"], np.float32)),
        w_gate=_stageT(_u8(g["L0_ffn_gate_w"])),
        w_up=_stageT(_u8(g["L0_ffn_up_w"])),
        w_down=_stageT(_u8(g["L0_ffn_down_w"])),
        v_ln1g=_veccol(g["L0_attn_a_ln1_g"]),
        v_ln1b=_veccol(g["L0_attn_a_ln1_b"]),
        v_ln2g=_veccol(g["L0_attn_a_ln2_g"]),
        v_ln2b=_veccol(g["L0_attn_a_ln2_b"]),
        v_u=_veccol(g["L0_attn_a_tm_u"]),
        v_mur=_veccol(g["L0_attn_a_tm_mu_r"]),
        v_muk=_veccol(g["L0_attn_a_tm_mu_k"]),
        v_muv=_veccol(g["L0_attn_a_tm_mu_v"]),
        v_cmur=_veccol(g["L0_attn_a_cm_mu_r"]),
        v_cmuk=_veccol(g["L0_attn_a_cm_mu_k"]),
        v_gsc=_veccol(g["L0_ffn_gate_scale"], 64),
        v_usc=_veccol(g["L0_ffn_up_scale"], 64),
        v_dsc=_veccol(g["L0_ffn_down_scale"]),
        v_blat=np.ascontiguousarray(np.asarray(g["L0_attn_b_b_lat"], np.float32).reshape(128, 1)),
        v_bd=_veccol(g["L0_attn_b_b_d"]),
        v_bproj=_veccol(g["L0_attn_b_b_proj"]),
        v_ident=np.eye(128).astype(ml_dtypes.bfloat16),
    )
    i_ = np.arange(128)[:, None]
    jq = np.arange(512)[None, :]
    tri = np.zeros((128, 4, 512), np.float32)
    for j2 in range(4):
        tri[:, j2, :] = (128 * j2 + i_ <= jq).astype(np.float32)
    shared["v_tri"] = np.ascontiguousarray(tri.reshape(128, 2048)).astype(ml_dtypes.bfloat16)
    in_maps = []
    for c in range(8):
        b, h = c // 2, c % 2
        t0 = h * TOK
        xe = np.zeros((D, WEXT), np.float32)
        lo = t0 - HALO
        src0 = max(lo, 0)
        xe[:, src0 - lo:] = x[b, src0:t0 + TOK, :].T
        btab = np.full((128, 89), NEGBIG, np.float32)
        for s in range(4):
            V = t0 + 512 * s
            for j in range(TS_ATT[s] - 4):
                tok = 128 * j + np.arange(128)
                btab[:, COLOFF[s] + j] = np.where(tok < V, -8.0, NEGBIG)
        btab[:, 88] = -8.0
        m = dict(shared)
        m["x_ext"] = np.ascontiguousarray(xe)
        m["v_hmask"] = np.full((128, 1), 0.0 if h == 0 else 1.0, np.float32)
        m["v_btab"] = np.ascontiguousarray(btab)
        in_maps.append(m)
    return in_maps


def kernel(**inputs):
    in_maps = _stage_inputs(inputs)
    nc = _build()
    res = run_bass_kernel_spmd(nc, in_maps, core_ids=list(range(8)))
    out = np.zeros((4, 4096, D), np.float32)
    for c in range(8):
        b, h = c // 2, c % 2
        out[b, h * TOK:(h + 1) * TOK, :] = np.asarray(res.results[c]["out"]).T
    return out


# version 05
# speedup vs baseline: 1.0666x; 1.0666x over previous
import numpy as np
import ml_dtypes
import concourse.bass as bass
import concourse.mybir as mybir
from concourse.bass_utils import run_bass_kernel_spmd

F32 = mybir.dt.float32
BF16 = mybir.dt.bfloat16
U8 = mybir.dt.uint8
AF = mybir.ActivationFunctionType
ALU = mybir.AluOpType

D = 2048
NCH = 16
TOK = 2048
HALO = 128
WEXT = TOK + HALO
EPS = 1e-5
LN1E8 = 18.420680743952367
ESCALE = 0.08838834764831845
NEGBIG = -1e9

EXT_TILES = [(0, 512), (512, 512), (1024, 512), (1536, 512), (2048, 128)]
OWN_TILES = [(0, 512), (512, 512), (1024, 512), (1536, 512)]
HALVES = [[(0, 512), (512, 512), (1024, 128)], [(1152, 512), (1664, 512)]]
HALF_RANGE = [(0, 1152), (1152, 1024)]

TS_ATT = [20, 24, 28, 32]
COLOFF = [0, 16, 36, 60]

ENGINES = ("sync", "tensor", "vector", "scalar", "gpsimd")


class Sem:
    __slots__ = ("h", "n", "name")
    def __init__(self, nc, name):
        self.h = nc.alloc_semaphore(name)
        self.n = 0
        self.name = name


class KB:
    def __init__(self, nc):
        self.nc = nc
        self.ops = {e: [] for e in ENGINES}
        self._sid = 0

    def sem(self, name):
        self._sid += 1
        return Sem(self.nc, f"{name}{self._sid}")

    def emit(self, eng, fn):
        self.ops[eng].append(fn)

    def wait(self, eng, sem, n=None):
        n = sem.n if n is None else n
        if n > 0:
            self.emit(eng, lambda e, h=sem.h, n=n: e.wait_ge(h, n))
        return n

    def op(self, eng, build, sig=None, k=1):
        if sig is None:
            self.emit(eng, build)
            return None
        sig.n += k
        n = sig.n
        self.emit(eng, lambda e, b=build, h=sig.h, k=k: b(e).then_inc(h, k))
        return n

    def dma(self, eng, dst, src, sig):
        return self.op(eng, lambda e, d=dst, s=src: e.dma_start(out=d, in_=s), sig, 16)

    def finalize(self):
        with self.nc.Block() as blk:
            for eng in ENGINES:
                lst = self.ops[eng]
                getattr(blk, eng)(lambda e, lst=lst: [f(e) for f in lst])



def _u8(w):
    w = np.ascontiguousarray(np.asarray(w, dtype=np.int32))
    return w.view(np.uint8).reshape(*w.shape, 4)[..., 0]

def _stageT(w):
    dout, c = w.shape
    S, A = dout // 128, c // 128
    return np.ascontiguousarray(w.reshape(S, 128, A, 128).transpose(0, 3, 2, 1))

def _stageN(w):
    c, dout = w.shape
    A, S = c // 128, dout // 128
    return np.ascontiguousarray(w.reshape(A, 128, S, 128).transpose(2, 1, 0, 3))

def _veccol(v, nch=NCH):
    return np.ascontiguousarray(np.asarray(v, dtype=np.float32).reshape(nch, 128).T)



class Core:
    def __init__(self, nc, tap=None):
        self.nc = nc
        self.kb = KB(nc)
        self.tap = tap
        self._declare_io()
        self._alloc()

    def _declare_io(self):
        nc = self.nc
        dt_ = nc.dram_tensor
        self.x_ext = dt_("x_ext", [D, WEXT], F32, kind="ExternalInput")
        self.w_tmr = dt_("w_tmr", [16, 128, 16, 128], F32, kind="ExternalInput")
        self.w_tmk = dt_("w_tmk", [16, 128, 16, 128], F32, kind="ExternalInput")
        self.w_tmv = dt_("w_tmv", [16, 128, 16, 128], F32, kind="ExternalInput")
        self.w_tmo = dt_("w_tmo", [16, 128, 16, 128], F32, kind="ExternalInput")
        self.w_cmr = dt_("w_cmr", [16, 128, 16, 128], F32, kind="ExternalInput")
        self.w_cmk = dt_("w_cmk", [16, 128, 16, 128], F32, kind="ExternalInput")
        self.w_cmv = dt_("w_cmv", [16, 128, 16, 128], F32, kind="ExternalInput")
        self.w_lat = dt_("w_lat", [1, 128, 16, 128], F32, kind="ExternalInput")
        self.w_d = dt_("w_d", [16, 128, 16, 128], F32, kind="ExternalInput")
        self.w_proj = dt_("w_proj", [16, 128, 16, 128], F32, kind="ExternalInput")
        self.w_gate = dt_("w_gate", [64, 128, 16, 128], U8, kind="ExternalInput")
        self.w_up = dt_("w_up", [64, 128, 16, 128], U8, kind="ExternalInput")
        self.w_down = dt_("w_down", [16, 128, 64, 128], U8, kind="ExternalInput")
        for nm, sh in [("v_ln1g", 16), ("v_ln1b", 16), ("v_ln2g", 16), ("v_ln2b", 16),
                       ("v_u", 16), ("v_mur", 16), ("v_muk", 16), ("v_muv", 16),
                       ("v_cmur", 16), ("v_cmuk", 16), ("v_gsc", 64), ("v_usc", 64),
                       ("v_dsc", 16), ("v_bd", 16), ("v_bproj", 16)]:
            setattr(self, nm, dt_(nm, [128, sh], F32, kind="ExternalInput"))
        self.v_blat = dt_("v_blat", [128, 1], F32, kind="ExternalInput")
        self.v_hmask = dt_("v_hmask", [128, 1], F32, kind="ExternalInput")
        self.v_ident = dt_("v_ident", [128, 128], BF16, kind="ExternalInput")
        self.v_tri = dt_("v_tri", [128, 2048], BF16, kind="ExternalInput")
        self.v_btab = dt_("v_btab", [128, 89], F32, kind="ExternalInput")
        self.tmout_d = dt_("tmout_d", [D, WEXT], F32, kind="Internal")
        self.xb_d = dt_("xb_d", [D, WEXT], F32, kind="Internal")
        self.x1_d = dt_("x1_d", [D, TOK], F32, kind="Internal")
        self.x2_d = dt_("x2_d", [D, TOK], F32, kind="Internal")
        self.x3_d = dt_("x3_d", [D, TOK], F32, kind="Internal")
        self.rows_d = dt_("rows_d", [2, WEXT], F32, kind="Internal")
        self.kcc_in = dt_("kcc_in", [128, TOK], BF16, kind="Internal")
        self.kcc_out = dt_("kcc_out", [256, TOK], BF16, kind="Internal")
        self.out = dt_("out", [D, TOK], F32, kind="ExternalOutput")
        if self.tap:
            shp = {"h1": [D, WEXT], "tmout": [D, WEXT], "xb": [D, WEXT],
                   "h2": [D, WEXT], "x1": [D, TOK], "x2": [D, TOK],
                   "x3": [D, TOK], "qk": [D + 128, TOK],
                   "yatt": [D, TOK], "kfull": [256, 4096]}[self.tap]
            self.tap_d = dt_("tap_d", shp, F32, kind="ExternalOutput")

    def _alloc(self):
        nc = self.nc
        A = nc.alloc_sbuf_tensor
        for nm, w in [("t_ln1g", 16), ("t_ln1b", 16), ("t_ln2g", 16), ("t_ln2b", 16),
                      ("t_ubias", 16), ("t_mur", 16), ("t_muk", 16), ("t_muv", 16),
                      ("t_cmur", 16), ("t_cmuk", 16), ("t_gsc", 64), ("t_usc", 64),
                      ("t_dsc", 16), ("t_bd", 16), ("t_bproj", 16)]:
            setattr(self, nm, A(nm, [128, w], F32))
        self.t_blat = A("t_blat", [128, 1], F32)
        self.t_hmask = A("t_hmask", [128, 1], F32)
        self.t_ident = A("t_ident", [128, 128], BF16)
        self.t_tri = A("t_tri", [128, 2048], BF16)
        self.t_btab = A("t_btab", [128, 89], F32)
        self.t_consts = A("t_consts", [128, 4], F32)
        self.t_ones_col = A("t_ones_col", [128, 1], BF16)
        self.t_ones_row_f = A("t_ones_row_f", [1, 128], F32)
        self.t_n128_row = A("t_n128_row", [1, 128], BF16)
        self.t_rows = A("t_rows", [1, 3072], F32)
        P = nc.alloc_psum_tensor
        self.ps = [P(f"ps{i}", [128, 512], F32) for i in range(4)]
        self.ps_r0 = P("ps_r0", [1, 512], F32)
        self.ps_r1 = P("ps_r1", [1, 512], F32)
        self.ps_b0 = P("ps_b0", [128, 512], F32)
        self.ps_tb = P("ps_tb", [128, 1024], BF16)
        rem = nc.sbuf_bytes_remaining
        self.AR_B = rem - 1024
        self.arena = A("arena", [128, self.AR_B], U8)
        assert self.AR_B >= 189000, f"arena too small: {self.AR_B}"

    def carve(self, off, n, dt):
        esz = {F32: 4, BF16: 2, U8: 1}[dt]
        assert off + n * esz <= self.AR_B, (off, n, esz, self.AR_B)
        return self.arena[:, off:off + n * esz].bitcast(dt)

    def build(self):
        kb = self.kb
        self.sW = kb.sem("wdma")
        self.sX = kb.sem("xdma")
        self.sS = kb.sem("store")
        self.sP = kb.sem("pe")
        self.sV = kb.sem("dve")
        self.sA = kb.sem("act")
        self.sC = kb.sem("cc")
        self._bank_free = [None] * 4
        self._bank_i = 0
        self.ph0_init()
        self.ph1_ln1()
        if self.tap == "h1":
            self.tap_out_hbuf(); kb.finalize(); return
        self.ph2_tm()
        if self.tap == "tmout":
            self.tap_dram(self.tmout_d, WEXT); kb.finalize(); return
        self.ph3_ln2()
        if self.tap == "xb":
            self.tap_dram(self.xb_d, WEXT); kb.finalize(); return
        if self.tap == "h2":
            self.tap_out_hbuf(); kb.finalize(); return
        self.ph4_cm()
        if self.tap == "x1":
            self.tap_dram(self.x1_d, TOK); kb.finalize(); return
        self.ph5_ffn(self.x1_d, self.x2_d)
        if self.tap == "x2":
            self.tap_dram(self.x2_d, TOK); kb.finalize(); return
        self.ph6_attnb()
        if self.tap in ("qk", "yatt", "kfull"):
            kb.finalize(); return
        if self.tap == "x3":
            self.tap_dram(self.x3_d, TOK); kb.finalize(); return
        self.ph5_ffn(self.x3_d, self.out)
        kb.wait("sync", self.sS)
        kb.finalize()

    def tap_out_hbuf(self):
        kb = self.kb
        hb = self.carve(0, NCH * 2177, BF16)
        tbuf = self.carve(100 * 1024, 512, F32)
        kb.wait("vector", self.sA)
        for (c0, w) in EXT_TILES:
            for ci in range(NCH):
                src = hb[:, ci * 2177 + 1 + c0: ci * 2177 + 1 + c0 + w]
                nV = kb.op("vector", lambda e, s=src, d=tbuf[:, 0:w]: e.tensor_copy(d, s), self.sV)
                kb.wait("sync", self.sV, nV)
                nS = kb.dma("sync", self.tap_d[ci * 128:(ci + 1) * 128, c0:c0 + w],
                            tbuf[:, 0:w], self.sS)
                kb.wait("vector", self.sS, nS)
        kb.wait("sync", self.sS)

    def tap_dram(self, src_d, wtot):
        kb = self.kb
        tbuf = self.carve(100 * 1024, 512, F32)
        kb.wait("sync", self.sS)
        for ci in range(NCH):
            for c0 in range(0, wtot, 512):
                w = min(512, wtot - c0)
                nL = kb.dma("sync", tbuf[:, 0:w], src_d[ci * 128:(ci + 1) * 128, c0:c0 + w], self.sX)
                kb.wait("sync", self.sX, nL)
                nS = kb.dma("sync", self.tap_d[ci * 128:(ci + 1) * 128, c0:c0 + w], tbuf[:, 0:w], self.sS)
                kb.wait("sync", self.sS, nS)
        kb.wait("sync", self.sS)

    def ph0_init(self):
        kb = self.kb
        pairs = [(self.t_ln1g, self.v_ln1g), (self.t_ln1b, self.v_ln1b),
                 (self.t_ln2g, self.v_ln2g), (self.t_ln2b, self.v_ln2b),
                 (self.t_ubias, self.v_u), (self.t_mur, self.v_mur),
                 (self.t_muk, self.v_muk), (self.t_muv, self.v_muv),
                 (self.t_cmur, self.v_cmur), (self.t_cmuk, self.v_cmuk),
                 (self.t_gsc, self.v_gsc), (self.t_usc, self.v_usc),
                 (self.t_dsc, self.v_dsc), (self.t_blat, self.v_blat),
                 (self.t_bd, self.v_bd), (self.t_bproj, self.v_bproj),
                 (self.t_hmask, self.v_hmask), (self.t_ident, self.v_ident),
                 (self.t_tri, self.v_tri), (self.t_btab, self.v_btab)]
        for dst, src in pairs:
            kb.dma("sync", dst[:, :], src[:, :], self.sX)
        c = self.t_consts
        kb.op("vector", lambda e: e.memset(c[:, 0:1], 1.0 / 2048.0), self.sV)
        kb.op("vector", lambda e: e.memset(c[:, 1:2], EPS), self.sV)
        kb.op("vector", lambda e: e.memset(c[:, 2:3], LN1E8), self.sV)
        kb.op("vector", lambda e: e.memset(self.t_ones_col[:, :], 1.0), self.sV)
        kb.op("vector", lambda e: e.memset(self.t_ones_row_f[:, :], 1.0), self.sV)
        kb.op("vector", lambda e: e.memset(self.t_n128_row[:, :], -128.0), self.sV)
        kb.wait("vector", self.sX)
        kb.op("vector", lambda e: e.tensor_scalar_add(
            self.t_ubias[:, :], self.t_ubias[:, :], c[:, 2:3]), self.sV)

    def _load_xtile(self, dram, c0, w, off, sig):
        kb = self.kb
        dst = self.carve(off, NCH * 512, F32)
        for g in range(4):
            src = dram[g * 512:(g + 1) * 512, c0:c0 + w].rearrange("(a p) q -> p a q", p=128)
            d = dst[:, g * 2048:(g + 1) * 2048].rearrange("p (a q) -> p a q", a=4)[:, :, 0:w]
            kb.dma("sync", d, src, sig)
        return dst

    def _store_xtile(self, dram, c0, w, off, sig):
        kb = self.kb
        src = self.carve(off, NCH * 512, F32)
        for g in range(4):
            s = src[:, g * 2048:(g + 1) * 2048].rearrange("p (a q) -> p a q", a=4)[:, :, 0:w]
            d = dram[g * 512:(g + 1) * 512, c0:c0 + w].rearrange("(a p) q -> p a q", p=128)
            kb.dma("sync", d, s, sig)

    def _stats(self, get_chunk, w, off_sc, want_invA=False):
        kb, c = self.kb, self.t_consts
        sP, sV, sA = self.sP, self.sV, self.sA
        r = self.t_rows
        rows = dict(m=r[0:1, 0:w], e2=r[0:1, 512:512 + w], var=r[0:1, 1024:1024 + w],
                    inv=r[0:1, 1536:1536 + w], invc=r[0:1, 2048:2048 + w],
                    tmp=r[0:1, 2560:2560 + w])
        kb.wait("tensor", sV)
        kb.wait("scalar", sV)
        mm_n = [0, 0, 0, 0]
        for ci in range(NCH):
            sl = ci % 2
            xa = get_chunk(ci)
            xbf = self.carve(off_sc + sl * 2048, 512, BF16)[:, 0:w]
            xsq = self.carve(off_sc + 4096 + sl * 2048, 512, BF16)[:, 0:w]
            if mm_n[sl]:
                kb.wait("vector", sP, mm_n[sl])
            nV = kb.op("vector", lambda e, xa=xa, d=xbf: e.tensor_copy(d, xa), sV)
            if mm_n[2 + sl]:
                kb.wait("scalar", sP, mm_n[2 + sl])
            nA = kb.op("scalar", lambda e, xa=xa, d=xsq: e.activation(d, xa, AF.Square), sA)
            kb.wait("tensor", sV, nV)
            kb.wait("tensor", sA, nA)
            kb.op("tensor", lambda e, s=xbf, f=(ci == 0), l=(ci == NCH - 1):
                  e.matmul(self.ps_r0[:, 0:w], self.t_ones_col[:, :], s, start=f, stop=l), sP)
            mm_n[sl] = sP.n
            kb.op("tensor", lambda e, s=xsq, f=(ci == 0), l=(ci == NCH - 1):
                  e.matmul(self.ps_r1[:, 0:w], self.t_ones_col[:, :], s, start=f, stop=l), sP)
            mm_n[2 + sl] = sP.n
        kb.wait("vector", sP)
        kb.op("vector", lambda e: e.tensor_scalar_mul(rows["m"], self.ps_r0[0:1, 0:w], c[0:1, 0:1]), sV)
        kb.op("vector", lambda e: e.tensor_scalar_mul(rows["e2"], self.ps_r1[0:1, 0:w], c[0:1, 0:1]), sV)
        kb.op("vector", lambda e: e.tensor_mul(rows["tmp"], rows["m"], rows["m"]), sV)
        kb.op("vector", lambda e: e.tensor_sub(rows["var"], rows["e2"], rows["tmp"]), sV)
        kb.op("vector", lambda e: e.tensor_scalar_add(rows["tmp"], rows["var"], c[0:1, 1:2]), sV)
        nV = kb.op("vector", lambda e: e.reciprocal(rows["inv"], rows["tmp"]), sV)
        kb.wait("scalar", sV, nV)
        nA = kb.op("scalar", lambda e: e.activation(rows["inv"], rows["inv"], AF.Sqrt), sA)
        kb.wait("vector", sA, nA)
        if want_invA:
            kb.op("vector", lambda e: e.tensor_mul(rows["tmp"], rows["inv"], rows["inv"]), sV)
            kb.op("vector", lambda e: e.tensor_mul(rows["tmp"], rows["var"], rows["tmp"]), sV)
            kb.op("vector", lambda e: e.tensor_scalar_add(rows["tmp"], rows["tmp"], c[0:1, 1:2]), sV)
            nV = kb.op("vector", lambda e: e.reciprocal(rows["e2"], rows["tmp"]), sV)
            kb.wait("scalar", sV, nV)
            nA = kb.op("scalar", lambda e: e.activation(rows["e2"], rows["e2"], AF.Sqrt), sA)
            kb.wait("vector", sA, nA)
            kb.op("vector", lambda e: e.tensor_mul(rows["invc"], rows["inv"], rows["e2"]), sV)
        return rows

    def _bcast(self, row, w, off):
        kb = self.kb
        dst = self.carve(off, 512, F32)[:, 0:w]
        kb.wait("tensor", self.sV)
        kb.wait("tensor", self.sA)
        nP = kb.op("tensor", lambda e, r=row, w=w: e.matmul(
            self.ps_b0[:, 0:w], self.t_ones_row_f[:, :], r, start=True, stop=True), self.sP)
        kb.wait("vector", self.sP, nP)
        nV = kb.op("vector", lambda e, d=dst, w=w: e.tensor_copy(d, self.ps_b0[:, 0:w]), self.sV)
        return dst, nV

    def ph1_ln1(self):
        kb = self.kb
        hb = self.carve(0, NCH * 2177, BF16)
        O_XT, O_SC, O_MB, O_IB, O_TT = 69664, 102432, 110624, 112672, 114720
        for (c0, w) in EXT_TILES:
            kb.wait("sync", self.sV)
            kb.wait("sync", self.sA)
            kb.wait("sync", self.sS)
            xt = self._load_xtile(self.x_ext, c0, w, O_XT, self.sX)
            nL = self.sX.n
            kb.wait("vector", self.sX, nL)
            kb.wait("scalar", self.sX, nL)
            rows = self._stats(lambda ci: xt[:, ci * 512:ci * 512 + w], w, O_SC, want_invA=True)
            kb.wait("sync", self.sV)
            kb.wait("sync", self.sA)
            kb.dma("sync", self.rows_d[0:1, c0:c0 + w], rows["m"], self.sS)
            kb.dma("sync", self.rows_d[1:2, c0:c0 + w], rows["inv"], self.sS)
            m_sb, _ = self._bcast(rows["m"], w, O_MB)
            ic_sb, nV = self._bcast(rows["invc"], w, O_IB)
            kb.wait("vector", self.sV, nV)
            tts = [self.carve(O_TT, 512, F32)[:, 0:w],
                   self.carve(O_TT + 2048, 512, F32)[:, 0:w]]
            act_n = [0, 0]
            for ci in range(NCH):
                sl = ci % 2
                tt = tts[sl]
                xa = xt[:, ci * 512:ci * 512 + w]
                if act_n[sl]:
                    kb.wait("vector", self.sA, act_n[sl])
                kb.op("vector", lambda e, xa=xa, m=m_sb, d=tt: e.tensor_sub(d, xa, m), self.sV)
                nV = kb.op("vector", lambda e, d=tt, i=ic_sb: e.tensor_mul(d, d, i), self.sV)
                kb.wait("scalar", self.sV, nV)
                dst = hb[:, ci * 2177 + 1 + c0: ci * 2177 + 1 + c0 + w]
                act_n[sl] = kb.op("scalar", lambda e, s=tt, d=dst, ci=ci: e.activation(
                    d, s, AF.Identity, bias=self.t_ln1b[:, ci:ci + 1],
                    scale=self.t_ln1g[:, ci:ci + 1]), self.sA)
        kb.wait("vector", self.sA)
        for ci in range(NCH):
            kb.op("vector", lambda e, ci=ci: e.memset(hb[:, ci * 2177:ci * 2177 + 1], 0.0), self.sV)
            kb.op("vector", lambda e, ci=ci: e.tensor_scalar_mul(
                hb[:, ci * 2177 + 128:ci * 2177 + 129],
                hb[:, ci * 2177 + 128:ci * 2177 + 129], self.t_hmask[:, 0:1]), self.sV)

    def _mix(self, hb, mix, mu, h0, hw, o_dtmp):
        kb = self.kb
        dtmp = self.carve(o_dtmp, 1152, BF16)
        kb.wait("vector", self.sP)
        for ci in range(NCH):
            h = hb[:, ci * 2177 + 1 + h0: ci * 2177 + 1 + h0 + hw]
            hp = hb[:, ci * 2177 + h0: ci * 2177 + h0 + hw]
            d = dtmp[:, 0:hw]
            m = mix[:, ci * 1152:ci * 1152 + hw]
            kb.op("vector", lambda e, h=h, hp=hp, d=d: e.tensor_sub(d, h, hp), self.sV)
            kb.op("vector", lambda e, d=d, hp=hp, m=m, ci=ci, mu=mu: e.scalar_tensor_tensor(
                m, d, mu[:, ci:ci + 1], hp, ALU.mult, ALU.add), self.sV)
        kb.wait("tensor", self.sV)

    def ph2_tm(self):
        kb = self.kb
        hb = self.carve(0, NCH * 2177, BF16)
        O_MIX, O_RS = 69664, 106528
        O_W, O_SS, O_DT, O_OT = 143392, 151584, 153888, 156192
        kb.wait("tensor", self.sV)
        for hi, (h0, hw) in enumerate(HALF_RANGE):
            mix = self.carve(O_MIX, NCH * 1152, BF16)
            rsb = self.carve(O_RS, NCH * 1152, BF16)
            sstr = self.carve(O_SS, 1152, BF16)
            tiles = HALVES[hi]
            mixf = lambda a, ts0, tw: mix[:, a * 1152 + (ts0 - h0): a * 1152 + (ts0 - h0) + tw]
            rsf = lambda a, ts0, tw: rsb[:, a * 1152 + (ts0 - h0): a * 1152 + (ts0 - h0) + tw]
            self._mix(hb, mix, self.t_mur, h0, hw, O_DT)
            def evac_r(s, ts0, tw, bank):
                dst = rsb[:, s * 1152 + (ts0 - h0): s * 1152 + (ts0 - h0) + tw]
                nA = kb.op("scalar", lambda e, b=bank, d=dst, tw=tw: e.activation(
                    d, b[:, 0:tw], AF.Sigmoid), self.sA)
                return ("scalar", self.sA, nA)
            self.mm_sweep(self.w_tmr, mixf, tiles, evac_r, O_W)
            self._mix(hb, mix, self.t_muk, h0, hw, O_DT)
            def evac_k(s, ts0, tw, bank):
                dst = sstr[:, (ts0 - h0):(ts0 - h0) + tw]
                kb.wait("scalar", self.sV)
                nA = kb.op("scalar", lambda e, b=bank, d=dst, s=s, tw=tw: e.activation(
                    d, b[:, 0:tw], AF.Sigmoid, bias=self.t_ubias[:, s:s + 1]), self.sA)
                kb.wait("vector", self.sA, nA)
                r_ap = rsb[:, s * 1152 + (ts0 - h0): s * 1152 + (ts0 - h0) + tw]
                nV = kb.op("vector", lambda e, r=r_ap, d=dst: e.tensor_mul(r, d, r), self.sV)
                return ("vector", self.sV, nV)
            self.mm_sweep(self.w_tmk, mixf, tiles, evac_k, O_W)
            self._mix(hb, mix, self.t_muv, h0, hw, O_DT)
            def evac_v(s, ts0, tw, bank):
                r_ap = rsb[:, s * 1152 + (ts0 - h0): s * 1152 + (ts0 - h0) + tw]
                nV = kb.op("vector", lambda e, b=bank, r=r_ap, tw=tw: e.tensor_mul(
                    r, b[:, 0:tw], r), self.sV)
                return ("vector", self.sV, nV)
            self.mm_sweep(self.w_tmv, mixf, tiles, evac_v, O_W)
            ot = self.carve(O_OT, 1024, F32)
            def evac_o(s, ts0, tw, bank, slot=[0]):
                sl = slot[0]; slot[0] ^= 1
                dst = ot[:, sl * 512: sl * 512 + tw]
                kb.wait("vector", self.sS)
                nV = kb.op("vector", lambda e, b=bank, d=dst, tw=tw: e.tensor_copy(d, b[:, 0:tw]), self.sV)
                kb.wait("sync", self.sV, nV)
                kb.dma("sync", self.tmout_d[s * 128:(s + 1) * 128, ts0:ts0 + tw], dst, self.sS)
                return ("vector", self.sV, nV)
            self.mm_sweep(self.w_tmo, rsf, tiles, evac_o, O_W)

    def mm_sweep(self, wdram, rhs_fn, tiles, evac, o_w, nstrips=16, nacc=NCH, wbytes=4096):
        kb = self.kb
        kb.wait("gpsimd", self.sP)
        wsl = [self.carve(o_w + i * wbytes, wbytes // 2, BF16) for i in range(2)]
        wfree = [None, None]
        for s in range(nstrips):
            sl = s % 2
            if wfree[sl] is not None:
                kb.wait("gpsimd", wfree[sl][0], wfree[sl][1])
            nW = kb.dma("gpsimd", wsl[sl][:, :], wdram[s].rearrange("p a q -> p (a q)"), self.sW)
            kb.wait("tensor", self.sW, nW)
            for (ts0, tw) in tiles:
                bi = self._bank_i
                bank = self.ps[bi]
                if self._bank_free[bi] is not None:
                    _, sem, n = self._bank_free[bi]
                    kb.wait("tensor", sem, n)
                for a in range(nacc):
                    kb.op("tensor", lambda e, l=wsl[sl][:, a * 128:(a + 1) * 128],
                          r=rhs_fn(a, ts0, tw), b=bank, tw=tw,
                          f=(a == 0), la=(a == nacc - 1): e.matmul(
                              b[:, 0:tw], l, r, start=f, stop=la),
                          self.sP if a == nacc - 1 else None)
                nP = self.sP.n
                kb.wait("vector", self.sP, nP)
                kb.wait("scalar", self.sP, nP)
                self._bank_free[bi] = evac(s, ts0, tw, bank)
                self._bank_i = (bi + 1) % 4
            wfree[sl] = (self.sP, self.sP.n)

    def ph3_ln2(self):
        kb = self.kb
        hb = self.carve(0, NCH * 2177, BF16)
        O_XT, O_TM = 69664, 102432
        O_RW, O_MB, O_IB, O_SC, O_TT = 135200, 139296, 141344, 143392, 151584
        for (c0, w) in EXT_TILES:
            kb.wait("sync", self.sV)
            kb.wait("sync", self.sA)
            kb.wait("sync", self.sS)
            xt = self._load_xtile(self.x_ext, c0, w, O_XT, self.sX)
            tm = self._load_xtile(self.tmout_d, c0, w, O_TM, self.sX)
            rowbuf = self.carve(O_RW, 1024, F32)
            kb.dma("sync", rowbuf[0:1, 0:w], self.rows_d[0:1, c0:c0 + w], self.sX)
            kb.dma("sync", rowbuf[0:1, 512:512 + w], self.rows_d[1:2, c0:c0 + w], self.sX)
            nL = self.sX.n
            kb.wait("vector", self.sX, nL)
            kb.wait("scalar", self.sX, nL)
            kb.wait("tensor", self.sX, nL)
            m_sb, _ = self._bcast(rowbuf[0:1, 0:w], w, O_MB)
            i_sb, nV = self._bcast(rowbuf[0:1, 512:512 + w], w, O_IB)
            kb.wait("vector", self.sV, nV)
            tt = self.carve(O_TT, 512, F32)[:, 0:w]
            for ci in range(NCH):
                xa = xt[:, ci * 512:ci * 512 + w]
                ta = tm[:, ci * 512:ci * 512 + w]
                kb.op("vector", lambda e, xa=xa, m=m_sb, d=tt: e.tensor_sub(d, xa, m), self.sV)
                kb.op("vector", lambda e, d=tt, i=i_sb: e.tensor_mul(d, d, i), self.sV)
                kb.op("vector", lambda e, d=ta, a=tt: e.tensor_add(d, a, d), self.sV)
                nV = kb.op("vector", lambda e, d=xa, b=ta: e.tensor_add(d, b, d), self.sV)
            kb.wait("sync", self.sV, nV)
            self._store_xtile(self.xb_d, c0, w, O_XT, self.sS)
            rows = self._stats(lambda ci: tm[:, ci * 512:ci * 512 + w], w, O_SC)
            m2, _ = self._bcast(rows["m"], w, O_MB)
            i2, nV = self._bcast(rows["inv"], w, O_IB)
            kb.wait("vector", self.sV, nV)
            tts2 = [self.carve(O_TT, 512, F32)[:, 0:w],
                    self.carve(O_TT + 2048, 512, F32)[:, 0:w]]
            act_n = [0, 0]
            for ci in range(NCH):
                sl = ci % 2
                tt2 = tts2[sl]
                ta = tm[:, ci * 512:ci * 512 + w]
                if act_n[sl]:
                    kb.wait("vector", self.sA, act_n[sl])
                kb.op("vector", lambda e, xa=ta, m=m2, d=tt2: e.tensor_sub(d, xa, m), self.sV)
                nV = kb.op("vector", lambda e, d=tt2, i=i2: e.tensor_mul(d, d, i), self.sV)
                kb.wait("scalar", self.sV, nV)
                dst = hb[:, ci * 2177 + 1 + c0: ci * 2177 + 1 + c0 + w]
                act_n[sl] = kb.op("scalar", lambda e, s=tt2, d=dst, ci=ci: e.activation(
                    d, s, AF.Identity, bias=self.t_ln2b[:, ci:ci + 1],
                    scale=self.t_ln2g[:, ci:ci + 1]), self.sA)
        kb.wait("vector", self.sA)
        for ci in range(NCH):
            kb.op("vector", lambda e, ci=ci: e.memset(hb[:, ci * 2177:ci * 2177 + 1], 0.0), self.sV)
            kb.op("vector", lambda e, ci=ci: e.tensor_scalar_mul(
                hb[:, ci * 2177 + 128:ci * 2177 + 129],
                hb[:, ci * 2177 + 128:ci * 2177 + 129], self.t_hmask[:, 0:1]), self.sV)

    def ph4_cm(self):
        kb = self.kb
        hb = self.carve(0, NCH * 2177, BF16)
        O_MIX, O_K2 = 69664, 106528
        O_W, O_R2, O_TK, O_XB = 143392, 159776, 162080, 163104
        kb.wait("tensor", self.sV)
        kb.wait("sync", self.sS)
        for hi, (h0, hw) in enumerate(HALF_RANGE):
            mix = self.carve(O_MIX, NCH * 1152, BF16)
            k2b = self.carve(O_K2, NCH * 1152, BF16)
            tiles = HALVES[hi]
            mixf = lambda a, ts0, tw: mix[:, a * 1152 + (ts0 - h0): a * 1152 + (ts0 - h0) + tw]
            k2f = lambda a, ts0, tw: k2b[:, a * 1152 + (ts0 - h0): a * 1152 + (ts0 - h0) + tw]
            self._mix(hb, mix, self.t_cmuk, h0, hw, 171296)
            tk = self.carve(O_TK, 512, BF16)
            def evac_k2(s, ts0, tw, bank):
                kb.wait("vector", self.sA)
                nV = kb.op("vector", lambda e, b=bank, d=tk[:, 0:tw], tw=tw:
                           e.tensor_scalar_max(d, b[:, 0:tw], 0.0), self.sV)
                kb.wait("scalar", self.sV, nV)
                dst = k2b[:, s * 1152 + (ts0 - h0): s * 1152 + (ts0 - h0) + tw]
                nA = kb.op("scalar", lambda e, s_=tk[:, 0:tw], d=dst: e.activation(
                    d, s_, AF.Square), self.sA)
                return ("scalar", self.sA, nA)
            self.mm_sweep(self.w_cmk, mixf, tiles, evac_k2, O_W)
            self._mix(hb, mix, self.t_cmur, h0, hw, 171296)
            r2t = self.carve(O_R2, 1152, BF16)
            xbsl = self.carve(O_XB, 2048, F32)
            kb.wait("gpsimd", self.sP)
            wR = [self.carve(O_W + i * 4096, 2048, BF16) for i in range(2)]
            wV = [self.carve(O_W + 8192 + i * 4096, 2048, BF16) for i in range(2)]
            wfree = [None, None]
            for s in range(16):
                sl = s % 2
                if wfree[sl] is not None:
                    kb.wait("gpsimd", wfree[sl][0], wfree[sl][1])
                kb.dma("gpsimd", wR[sl][:, :], self.w_cmr[s].rearrange("p a q -> p (a q)"), self.sW)
                nW = kb.dma("gpsimd", wV[sl][:, :], self.w_cmv[s].rearrange("p a q -> p (a q)"), self.sW)
                kb.wait("tensor", self.sW, nW)
                for (ts0, tw) in tiles:
                    bi = self._bank_i
                    bank = self.ps[bi]
                    if self._bank_free[bi] is not None:
                        _, sem, n = self._bank_free[bi]
                        kb.wait("tensor", sem, n)
                    for a in range(NCH):
                        kb.op("tensor", lambda e, l=wR[sl][:, a * 128:(a + 1) * 128],
                              r=mixf(a, ts0, tw), b=bank, tw=tw, f=(a == 0), la=(a == NCH - 1):
                              e.matmul(b[:, 0:tw], l, r, start=f, stop=la),
                              self.sP if a == NCH - 1 else None)
                    nPr = self.sP.n
                    kb.wait("scalar", self.sP, nPr)
                    kb.wait("scalar", self.sV)
                    dR = r2t[:, (ts0 - h0):(ts0 - h0) + tw]
                    nA = kb.op("scalar", lambda e, b=bank, d=dR, tw=tw: e.activation(
                        d, b[:, 0:tw], AF.Sigmoid), self.sA)
                    self._bank_free[bi] = ("scalar", self.sA, nA)
                    self._bank_i = (bi + 1) % 4
                    bi = self._bank_i
                    bank2 = self.ps[bi]
                    if self._bank_free[bi] is not None:
                        _, sem, n = self._bank_free[bi]
                        kb.wait("tensor", sem, n)
                    kb.wait("sync", self.sV)
                    xsl = xbsl[:, 0:512][:, 0:tw]
                    nX = kb.dma("sync", xsl, self.xb_d[s * 128:(s + 1) * 128, ts0:ts0 + tw], self.sX)
                    for a in range(NCH):
                        kb.op("tensor", lambda e, l=wV[sl][:, a * 128:(a + 1) * 128],
                              r=k2f(a, ts0, tw), b=bank2, tw=tw, f=(a == 0), la=(a == NCH - 1):
                              e.matmul(b[:, 0:tw], l, r, start=f, stop=la),
                              self.sP if a == NCH - 1 else None)
                    nP = self.sP.n
                    kb.wait("vector", self.sP, nP)
                    kb.wait("vector", self.sA, nA)
                    kb.wait("vector", self.sX, nX)
                    kb.wait("vector", self.sS)
                    ot = xbsl[:, 1024:1536][:, 0:tw]
                    kb.op("vector", lambda e, b=bank2, r=dR, d=ot, tw=tw: e.tensor_mul(
                        d, b[:, 0:tw], r), self.sV)
                    nV = kb.op("vector", lambda e, d=ot, x=xsl: e.tensor_add(d, d, x), self.sV)
                    self._bank_free[bi] = ("vector", self.sV, nV)
                    self._bank_i = (bi + 1) % 4
                    kb.wait("sync", self.sV, nV)
                    o0 = max(ts0, HALO)
                    if o0 < ts0 + tw:
                        off = o0 - ts0
                        kb.dma("sync", self.x1_d[s * 128:(s + 1) * 128, o0 - HALO:ts0 + tw - HALO],
                               ot[:, off:tw], self.sS)
                wfree[sl] = (self.sP, self.sP.n)

    def ph5_ffn(self, src_d, dst_d):
        kb = self.kb
        mid = self.carve(0, 64 * 512, BF16)
        h3 = self.carve(65536, NCH * 512, BF16)
        rt = self.carve(81920, 1024, BF16)
        sgt = self.carve(83968, 1024, BF16)
        ot = self.carve(86016, 1024, F32)
        O_XT = 90112
        O_GU, O_DN = 122880, 139264
        O_MB, O_IB, O_TT, O_SC = 172032, 174080, 176128, 178176
        kb.wait("gpsimd", self.sP)
        wg = [self.carve(O_GU + i * 4096, 2048, BF16) for i in range(2)]
        wu = [self.carve(O_GU + 8192 + i * 4096, 2048, BF16) for i in range(2)]
        wdn = [self.carve(O_DN + i * 16384, 8192, BF16) for i in range(2)]
        wfree_g = [None, None]; wfree_u = [None, None]; wfree_d = [None, None]
        sg_free = [0, 0]; od_free = [0, 0]
        for (c0, w) in OWN_TILES:
            kb.wait("sync", self.sV)
            kb.wait("sync", self.sA)
            kb.wait("sync", self.sS)
            xt = self._load_xtile(src_d, c0, w, O_XT, self.sX)
            nL = self.sX.n
            kb.wait("vector", self.sX, nL)
            kb.wait("scalar", self.sX, nL)
            rows = self._stats(lambda ci: xt[:, ci * 512:ci * 512 + w], w, O_SC)
            m_sb, _ = self._bcast(rows["m"], w, O_MB)
            i_sb, nV = self._bcast(rows["inv"], w, O_IB)
            kb.wait("vector", self.sV, nV)
            tt = self.carve(O_TT, 512, F32)[:, 0:w]
            for ci in range(NCH):
                xa = xt[:, ci * 512:ci * 512 + w]
                kb.op("vector", lambda e, xa=xa, m=m_sb, d=tt: e.tensor_sub(d, xa, m), self.sV)
                kb.op("vector", lambda e, d=h3[:, ci * 512:ci * 512 + w], s=tt, i=i_sb:
                      e.tensor_mul(d, s, i), self.sV)
            nVh = self.sV.n
            kb.wait("tensor", self.sV, nVh)
            for a in range(NCH):
                kb.op("tensor", lambda e, s=h3[:, a * 512:a * 512 + w], f=(a == 0), la=(a == NCH - 1):
                      e.matmul(self.ps_r0[:, 0:w], self.t_ones_col[:, :], s, start=f, stop=la),
                      self.sP if a == NCH - 1 else None)
            kb.wait("vector", self.sP)
            nV = kb.op("vector", lambda e: e.tensor_copy(rt[0:1, 0:w], self.ps_r0[0:1, 0:w]), self.sV)
            kb.wait("tensor", self.sV, nV)
            for ic in range(64):
                sl = ic % 2
                if wfree_g[sl] is not None:
                    kb.wait("gpsimd", wfree_g[sl][0], wfree_g[sl][1])
                kb.dma("gpsimd", wg[sl][:, :], self.w_gate[ic].rearrange("p a q -> p (a q)"), self.sW)
                if wfree_u[sl] is not None:
                    kb.wait("gpsimd", wfree_u[sl][0], wfree_u[sl][1])
                nWu = kb.dma("gpsimd", wu[sl][:, :], self.w_up[ic].rearrange("p a q -> p (a q)"), self.sW)
                kb.wait("tensor", self.sW, nWu)
                bi = self._bank_i
                bank_g = self.ps[bi]
                if self._bank_free[bi] is not None:
                    _, sem, n = self._bank_free[bi]
                    kb.wait("tensor", sem, n)
                for a in range(NCH):
                    kb.op("tensor", lambda e, l=wg[sl][:, a * 128:(a + 1) * 128],
                          r=h3[:, a * 512:a * 512 + w], b=bank_g, w=w, f=(a == 0):
                          e.matmul(b[:, 0:w], l, r, start=f, stop=False), None)
                kb.op("tensor", lambda e, b=bank_g, w=w: e.matmul(
                    b[:, 0:w], self.t_n128_row[:, :], rt[0:1, 0:w], start=False, stop=True), self.sP)
                nPg = self.sP.n
                big = bi
                self._bank_i = (bi + 1) % 4
                bi = self._bank_i
                bank_u = self.ps[bi]
                if self._bank_free[bi] is not None:
                    _, sem, n = self._bank_free[bi]
                    kb.wait("tensor", sem, n)
                for a in range(NCH):
                    kb.op("tensor", lambda e, l=wu[sl][:, a * 128:(a + 1) * 128],
                          r=h3[:, a * 512:a * 512 + w], b=bank_u, w=w, f=(a == 0):
                          e.matmul(b[:, 0:w], l, r, start=f, stop=False), None)
                kb.op("tensor", lambda e, b=bank_u, w=w: e.matmul(
                    b[:, 0:w], self.t_n128_row[:, :], rt[0:1, 0:w], start=False, stop=True), self.sP)
                nPu = self.sP.n
                kb.wait("scalar", self.sP, nPg)
                if sg_free[sl]:
                    kb.wait("scalar", self.sV, sg_free[sl])
                sg = sgt[:, sl * 512:sl * 512 + w]
                nA = kb.op("scalar", lambda e, b=bank_g, d=sg, ic=ic, w=w: e.activation(
                    d, b[:, 0:w], AF.Silu, scale=self.t_gsc[:, ic:ic + 1]), self.sA)
                self._bank_free[big] = ("scalar", self.sA, nA)
                kb.wait("vector", self.sP, nPu)
                kb.wait("vector", self.sA, nA)
                nV = kb.op("vector", lambda e, b=bank_u, s=sg, d=mid[:, ic * 512:ic * 512 + w],
                           ic=ic, w=w: e.scalar_tensor_tensor(
                               d, b[:, 0:w], self.t_usc[:, ic:ic + 1], s, ALU.mult, ALU.mult), self.sV)
                sg_free[sl] = nV
                self._bank_free[bi] = ("vector", self.sV, nV)
                self._bank_i = (bi + 1) % 4
                wfree_g[sl] = (self.sP, nPg)
                wfree_u[sl] = (self.sP, nPu)
            nVm = self.sV.n
            kb.wait("tensor", self.sV, nVm)
            for a in range(64):
                kb.op("tensor", lambda e, s=mid[:, a * 512:a * 512 + w], f=(a == 0), la=(a == 63):
                      e.matmul(self.ps_r0[:, 0:w], self.t_ones_col[:, :], s, start=f, stop=la),
                      self.sP if a == 63 else None)
            kb.wait("vector", self.sP)
            nV = kb.op("vector", lambda e: e.tensor_copy(rt[0:1, 512:512 + w], self.ps_r0[0:1, 0:w]), self.sV)
            kb.wait("tensor", self.sV, nV)
            for s in range(16):
                sl = s % 2
                if wfree_d[sl] is not None:
                    kb.wait("gpsimd", wfree_d[sl][0], wfree_d[sl][1])
                nWd = kb.dma("gpsimd", wdn[sl][:, :], self.w_down[s].rearrange("p a q -> p (a q)"), self.sW)
                kb.wait("tensor", self.sW, nWd)
                bi = self._bank_i
                bank = self.ps[bi]
                if self._bank_free[bi] is not None:
                    _, sem, n = self._bank_free[bi]
                    kb.wait("tensor", sem, n)
                for a in range(64):
                    kb.op("tensor", lambda e, l=wdn[sl][:, a * 128:(a + 1) * 128],
                          r=mid[:, a * 512:a * 512 + w], b=bank, w=w, f=(a == 0):
                          e.matmul(b[:, 0:w], l, r, start=f, stop=False), None)
                kb.op("tensor", lambda e, b=bank, w=w: e.matmul(
                    b[:, 0:w], self.t_n128_row[:, :], rt[0:1, 512:512 + w], start=False, stop=True), self.sP)
                nP = self.sP.n
                kb.wait("vector", self.sP, nP)
                if od_free[sl]:
                    kb.wait("vector", self.sS, od_free[sl])
                od = ot[:, sl * 512:sl * 512 + w]
                nV = kb.op("vector", lambda e, b=bank, d=od, s=s, w=w,
                           x=xt[:, s * 512:s * 512 + w]: e.scalar_tensor_tensor(
                               d, b[:, 0:w], self.t_dsc[:, s:s + 1], x, ALU.mult, ALU.add), self.sV)
                self._bank_free[bi] = ("vector", self.sV, nV)
                self._bank_i = (bi + 1) % 4
                kb.wait("sync", self.sV, nV)
                od_free[sl] = kb.dma("sync", dst_d[s * 128:(s + 1) * 128, c0:c0 + w], od, self.sS)
                wfree_d[sl] = (self.sP, nP)

    def ph6_attnb(self):
        kb = self.kb
        h3f = self.carve(0, NCH * 2048, BF16)
        qb = self.carve(65536, NCH * 2048, BF16)
        O_XT = 131072
        O_WSL = 163840
        k_own = self.carve(172032, 2048, BF16)
        O_MB, O_IB = 176128, 178176
        O_SC = 176128
        O_TT = 184320
        kb.wait("gpsimd", self.sP)
        for (c0, w) in OWN_TILES:
            kb.wait("sync", self.sV)
            kb.wait("sync", self.sA)
            kb.wait("sync", self.sS)
            xt = self._load_xtile(self.x2_d, c0, w, O_XT, self.sX)
            nL = self.sX.n
            kb.wait("vector", self.sX, nL)
            kb.wait("scalar", self.sX, nL)
            rows = self._stats(lambda ci: xt[:, ci * 512:ci * 512 + w], w, O_SC)
            m_sb, _ = self._bcast(rows["m"], w, O_MB)
            i_sb, nV = self._bcast(rows["inv"], w, O_IB)
            kb.wait("vector", self.sV, nV)
            tt = self.carve(O_TT, 512, F32)[:, 0:w]
            for ci in range(NCH):
                xa = xt[:, ci * 512:ci * 512 + w]
                kb.op("vector", lambda e, xa=xa, m=m_sb, d=tt: e.tensor_sub(d, xa, m), self.sV)
                kb.op("vector", lambda e, d=h3f[:, ci * 2048 + c0:ci * 2048 + c0 + w], s=tt, i=i_sb:
                      e.tensor_mul(d, s, i), self.sV)
        nVh = self.sV.n
        kb.wait("tensor", self.sV, nVh)
        h3ff = lambda a, c0, w: h3f[:, a * 2048 + c0:a * 2048 + c0 + w]
        def evac_q(s, c0, w, bank):
            nA = kb.op("scalar", lambda e, b=bank, d=qb[:, s * 2048 + c0:s * 2048 + c0 + w],
                       s_=s, w=w: e.activation(d, b[:, 0:w], AF.Identity,
                                               bias=self.t_bd[:, s_:s_ + 1]), self.sA)
            return ("scalar", self.sA, nA)
        self.mm_sweep(self.w_d, h3ff, OWN_TILES, evac_q, O_WSL)
        def evac_klat(s, c0, w, bank):
            nA = kb.op("scalar", lambda e, b=bank, d=k_own[:, c0:c0 + w], w=w: e.activation(
                d, b[:, 0:w], AF.Identity, bias=self.t_blat[:, 0:1]), self.sA)
            return ("scalar", self.sA, nA)
        self.mm_sweep(self.w_lat, h3ff, OWN_TILES, evac_klat, O_WSL, nstrips=1)
        if self.tap == "qk":
            self._tap_qk(qb, k_own)
            return
        kb.wait("sync", self.sA)
        nS = kb.dma("sync", self.kcc_in[:, :], k_own[:, :], self.sS)
        kb.wait("gpsimd", self.sS, nS)
        nC = kb.op("gpsimd", lambda e: e.collective_compute(
            "AllGather", ALU.bypass, replica_groups=[[0, 1], [2, 3], [4, 5], [6, 7]],
            ins=[self.kcc_in[:, :]], outs=[self.kcc_out[:, :]]), self.sC)
        kb.wait("sync", self.sC, nC)
        k_full = self.carve(O_XT, 4096, BF16)
        k_tm = self.carve(O_XT + 8192, 4096, BF16)
        k_tm_own = self.carve(O_XT + 16384, 2048, BF16)
        p_strip = self.carve(O_XT + 20480, 16 * 512, BF16)
        rb_sb = self.carve(167936, 512, F32)
        sums_row = self.carve(169984, 512, F32)
        kb.dma("sync", k_full[:, 0:2048], self.kcc_out[0:128, :], self.sX)
        nX = kb.dma("sync", k_full[:, 2048:4096], self.kcc_out[128:256, :], self.sX)
        kb.wait("tensor", self.sX, nX)
        def transpose_batch(src, n_tiles, dst):
            prev = 0
            for g in range(0, n_tiles, 4):
                nn = min(g + 4, n_tiles) - g
                if prev:
                    kb.wait("tensor", self.sV, prev)
                for j in range(g, g + nn):
                    kb.op("tensor", lambda e, s=src[:, j * 128:(j + 1) * 128],
                          d=self.ps_tb[:, (j % 4) * 128:(j % 4 + 1) * 128]:
                          e.transpose(d, s, self.t_ident[:, :]),
                          self.sP if j == g + nn - 1 else None)
                nP = self.sP.n
                kb.wait("vector", self.sP, nP)
                prev = kb.op("vector", lambda e, g=g, nn=nn, d=dst: e.tensor_copy(
                    d[:, g * 128:(g + nn) * 128], self.ps_tb[:, 0:nn * 128]), self.sV)
            kb.wait("tensor", self.sV)
        transpose_batch(k_full, 32, k_tm)
        transpose_batch(k_own, 16, k_tm_own)
        if self.tap == "kfull":
            kb.wait("vector", self.sV)
            tb3 = self.carve(184320, 1024, F32)
            for (c0, w) in [(0, 1024), (1024, 1024), (2048, 1024), (3072, 1024)]:
                nV = kb.op("vector", lambda e, c0=c0, w=w: e.tensor_copy(
                    tb3[:, 0:w], k_full[:, c0:c0 + w]), self.sV)
                kb.wait("sync", self.sV, nV)
                nS = kb.dma("sync", self.tap_d[0:128, c0:c0 + w], tb3[:, 0:w], self.sS)
                kb.wait("vector", self.sS, nS)
            for (c0, w) in [(0, 1024), (1024, 1024), (2048, 1024), (3072, 1024)]:
                nV = kb.op("vector", lambda e, c0=c0, w=w: e.tensor_copy(
                    tb3[:, 0:w], k_tm[:, c0:c0 + w]), self.sV)
                kb.wait("sync", self.sV, nV)
                nS = kb.dma("sync", self.tap_d[128:256, c0:c0 + w], tb3[:, 0:w], self.sS)
                kb.wait("vector", self.sS, nS)
            kb.wait("sync", self.sS)
            return
        kb.wait("tensor", self.sV)
        self._sums_free = None
        self._psb0_free = None
        import os
        lim = os.environ.get("DEV6D", "")
        nsl, nhd = (4, 16) if not lim else tuple(int(v) for v in lim.split(","))
        for si in range(nsl):
            Tn = TS_ATT[si]
            nint = Tn - 4
            for head in range(nhd):
                q_ap = qb[:, head * 2048 + si * 512: head * 2048 + (si + 1) * 512]
                y_bank = self.ps[2]
                if self._bank_free[2] is not None:
                    _, sem, n = self._bank_free[2]
                    kb.wait("tensor", sem, n)
                if self._sums_free is not None:
                    kb.wait("tensor", self._sums_free[1], self._sums_free[2])
                first_y = True
                first_sum = True
                rounds = [list(range(Tn))[i:i + 16] for i in range(0, Tn, 16)]
                for rnd in rounds:
                    pmask_n = 0
                    for idx, js in enumerate(rnd):
                        diag = js >= nint
                        jd = js - nint
                        bsl = idx % 2
                        bnk = self.ps[bsl]
                        if self._bank_free[bsl] is not None:
                            _, sem, n = self._bank_free[bsl]
                            kb.wait("tensor", sem, n)
                        if diag:
                            lhsT = k_own[:, si * 512 + jd * 128: si * 512 + (jd + 1) * 128]
                            bias = self.t_btab[:, 88:89]
                        else:
                            lhsT = k_full[:, js * 128:(js + 1) * 128]
                            bias = self.t_btab[:, COLOFF[si] + js:COLOFF[si] + js + 1]
                        nP = kb.op("tensor", lambda e, l=lhsT, r=q_ap, b=bnk: e.matmul(
                            b[:, :], l, r, start=True, stop=True), self.sP)
                        kb.wait("scalar", self.sP, nP)
                        pdst = p_strip[:, idx * 512:(idx + 1) * 512]
                        nA = kb.op("scalar", lambda e, b=bnk, d=pdst, bias=bias: e.activation(
                            d, b[:, :], AF.Exp, bias=bias, scale=ESCALE), self.sA)
                        self._bank_free[bsl] = ("scalar", self.sA, nA)
                        if diag:
                            kb.wait("vector", self.sA, nA)
                            pmask_n = kb.op("vector", lambda e, d=pdst, jd=jd: e.tensor_mul(
                                d, d, self.t_tri[:, jd * 512:(jd + 1) * 512]), self.sV)
                    kb.wait("tensor", self.sA)
                    if pmask_n:
                        kb.wait("tensor", self.sV, pmask_n)
                    for idx, js in enumerate(rnd):
                        kb.op("tensor", lambda e, p=p_strip[:, idx * 512:(idx + 1) * 512],
                              f=(first_sum and idx == 0), la=(js == Tn - 1):
                              e.matmul(self.ps_r0[:, :], self.t_ones_col[:, :], p, start=f, stop=la),
                              self.sP if js == Tn - 1 else None)
                    first_sum = False
                    for idx, js in enumerate(rnd):
                        diag = js >= nint
                        jd = js - nint
                        lhsT = (k_tm_own[:, (si * 4 + jd) * 128:(si * 4 + jd + 1) * 128] if diag
                                else k_tm[:, js * 128:(js + 1) * 128])
                        kb.op("tensor", lambda e, l=lhsT, p=p_strip[:, idx * 512:(idx + 1) * 512],
                              f=(first_y and idx == 0), la=(js == Tn - 1):
                              e.matmul(y_bank[:, :], l, p, start=f, stop=la),
                              self.sP if js == Tn - 1 else None)
                    first_y = False
                nP = self.sP.n
                kb.wait("vector", self.sP, nP)
                nVs = kb.op("vector", lambda e: e.tensor_copy(
                    sums_row[0:1, 0:512], self.ps_r0[0:1, :]), self.sV)
                self._sums_free = ("vector", self.sV, nVs)
                nV = kb.op("vector", lambda e: e.reciprocal(
                    sums_row[0:1, 0:512], sums_row[0:1, 0:512]), self.sV)
                kb.wait("tensor", self.sV, nV)
                if self._psb0_free is not None:
                    kb.wait("tensor", self._psb0_free[1], self._psb0_free[2])
                nP2 = kb.op("tensor", lambda e: e.matmul(
                    self.ps_b0[:, :], self.t_ones_row_f[:, :], sums_row[0:1, 0:512],
                    start=True, stop=True), self.sP)
                kb.wait("vector", self.sP, nP2)
                nVrb = kb.op("vector", lambda e: e.tensor_copy(rb_sb[:, :], self.ps_b0[:, :]), self.sV)
                self._psb0_free = ("vector", self.sV, nVrb)
                ydst = h3f[:, head * 2048 + si * 512: head * 2048 + (si + 1) * 512]
                nV = kb.op("vector", lambda e, d=ydst: e.tensor_mul(d, y_bank[:, :], rb_sb[:, :]), self.sV)
                self._bank_free[2] = ("vector", self.sV, nV)
        if self.tap == "yatt":
            kb.wait("vector", self.sV)
            tb2 = self.carve(184320, 512, F32)
            for s in range(16):
                for (c0, w) in OWN_TILES:
                    nV = kb.op("vector", lambda e, s_=s, c0=c0, w=w: e.tensor_copy(
                        tb2[:, 0:w], h3f[:, s_ * 2048 + c0:s_ * 2048 + c0 + w]), self.sV)
                    kb.wait("sync", self.sV, nV)
                    nS = kb.dma("sync", self.tap_d[s * 128:(s + 1) * 128, c0:c0 + w],
                                tb2[:, 0:w], self.sS)
                    kb.wait("vector", self.sS, nS)
            kb.wait("sync", self.sS)
            return
        nVy = self.sV.n
        kb.wait("tensor", self.sV, nVy)
        kb.wait("gpsimd", self.sP)
        x2sl = self.carve(65536, 1024, F32)
        oproj = self.carve(69632, 1024, F32)
        def evac_proj(s, c0, w, bank, st={"ti": 0}):
            ti = st["ti"]; st["ti"] = (ti + 1) % 2
            kb.wait("sync", self.sV)
            xs = x2sl[:, ti * 512:ti * 512 + w]
            nX = kb.dma("sync", xs, self.x2_d[s * 128:(s + 1) * 128, c0:c0 + w], self.sX)
            kb.wait("vector", self.sX, nX)
            kb.wait("vector", self.sS)
            od = oproj[:, ti * 512:ti * 512 + w]
            nV = kb.op("vector", lambda e, b=bank, d=od, s_=s, x=xs, w=w: e.scalar_tensor_tensor(
                d, b[:, 0:w], self.t_bproj[:, s_:s_ + 1], x, ALU.add, ALU.add), self.sV)
            kb.wait("sync", self.sV, nV)
            kb.dma("sync", self.x3_d[s * 128:(s + 1) * 128, c0:c0 + w], od, self.sS)
            return ("vector", self.sV, nV)
        yf = lambda a, c0, w: h3f[:, a * 2048 + c0:a * 2048 + c0 + w]
        self.mm_sweep(self.w_proj, yf, OWN_TILES, evac_proj, O_WSL)

    def _tap_qk(self, qb, k_own):
        kb = self.kb
        kb.wait("vector", self.sA)
        tb = self.carve(184320, 512, F32)
        for s in range(16):
            for (c0, w) in OWN_TILES:
                nV = kb.op("vector", lambda e, s_=s, c0=c0, w=w: e.tensor_copy(
                    tb[:, 0:w], qb[:, s_ * 2048 + c0:s_ * 2048 + c0 + w]), self.sV)
                kb.wait("sync", self.sV, nV)
                nS = kb.dma("sync", self.tap_d[s * 128:(s + 1) * 128, c0:c0 + w], tb[:, 0:w], self.sS)
                kb.wait("vector", self.sS, nS)
        for (c0, w) in OWN_TILES:
            nV = kb.op("vector", lambda e, c0=c0, w=w: e.tensor_copy(
                tb[:, 0:w], k_own[:, c0:c0 + w]), self.sV)
            kb.wait("sync", self.sV, nV)
            nS = kb.dma("sync", self.tap_d[2048:2176, c0:c0 + w], tb[:, 0:w], self.sS)
            kb.wait("vector", self.sS, nS)
        kb.wait("sync", self.sS)



def _build(tap=None):
    nc = bass.Bass("TRN2", target_bir_lowering=False)
    core = Core(nc, tap=tap)
    core.build()
    return nc


def _stage_inputs(inputs):
    g = inputs
    x = np.asarray(g["L0_attn_a_x"], dtype=np.float32)
    shared = dict(
        w_tmr=_stageT(np.asarray(g["L0_attn_a_tm_Wr"], np.float32)),
        w_tmk=_stageT(np.asarray(g["L0_attn_a_tm_Wk"], np.float32)),
        w_tmv=_stageT(np.asarray(g["L0_attn_a_tm_Wv"], np.float32)),
        w_tmo=_stageT(np.asarray(g["L0_attn_a_tm_Wout"], np.float32)),
        w_cmr=_stageT(np.asarray(g["L0_attn_a_cm_Wr"], np.float32)),
        w_cmk=_stageT(np.asarray(g["L0_attn_a_cm_Wk"], np.float32)),
        w_cmv=_stageT(np.asarray(g["L0_attn_a_cm_Wv"], np.float32)),
        w_lat=_stageN(np.asarray(g["L0_attn_b_W_lat"], np.float32)),
        w_d=_stageN(np.asarray(g["L0_attn_b_W_d"], np.float32)),
        w_proj=_stageN(np.asarray(g["L0_attn_b_W_proj"], np.float32)),
        w_gate=_stageT(_u8(g["L0_ffn_gate_w"])),
        w_up=_stageT(_u8(g["L0_ffn_up_w"])),
        w_down=_stageT(_u8(g["L0_ffn_down_w"])),
        v_ln1g=_veccol(g["L0_attn_a_ln1_g"]),
        v_ln1b=_veccol(g["L0_attn_a_ln1_b"]),
        v_ln2g=_veccol(g["L0_attn_a_ln2_g"]),
        v_ln2b=_veccol(g["L0_attn_a_ln2_b"]),
        v_u=_veccol(g["L0_attn_a_tm_u"]),
        v_mur=_veccol(g["L0_attn_a_tm_mu_r"]),
        v_muk=_veccol(g["L0_attn_a_tm_mu_k"]),
        v_muv=_veccol(g["L0_attn_a_tm_mu_v"]),
        v_cmur=_veccol(g["L0_attn_a_cm_mu_r"]),
        v_cmuk=_veccol(g["L0_attn_a_cm_mu_k"]),
        v_gsc=_veccol(g["L0_ffn_gate_scale"], 64),
        v_usc=_veccol(g["L0_ffn_up_scale"], 64),
        v_dsc=_veccol(g["L0_ffn_down_scale"]),
        v_blat=np.ascontiguousarray(np.asarray(g["L0_attn_b_b_lat"], np.float32).reshape(128, 1)),
        v_bd=_veccol(g["L0_attn_b_b_d"]),
        v_bproj=_veccol(g["L0_attn_b_b_proj"]),
        v_ident=np.eye(128).astype(ml_dtypes.bfloat16),
    )
    i_ = np.arange(128)[:, None]
    jq = np.arange(512)[None, :]
    tri = np.zeros((128, 4, 512), np.float32)
    for j2 in range(4):
        tri[:, j2, :] = (128 * j2 + i_ <= jq).astype(np.float32)
    shared["v_tri"] = np.ascontiguousarray(tri.reshape(128, 2048)).astype(ml_dtypes.bfloat16)
    in_maps = []
    for c in range(8):
        b, h = c // 2, c % 2
        t0 = h * TOK
        xe = np.zeros((D, WEXT), np.float32)
        lo = t0 - HALO
        src0 = max(lo, 0)
        xe[:, src0 - lo:] = x[b, src0:t0 + TOK, :].T
        btab = np.full((128, 89), NEGBIG, np.float32)
        for s in range(4):
            V = t0 + 512 * s
            for j in range(TS_ATT[s] - 4):
                tok = 128 * j + np.arange(128)
                btab[:, COLOFF[s] + j] = np.where(tok < V, -8.0, NEGBIG)
        btab[:, 88] = -8.0
        m = dict(shared)
        m["x_ext"] = np.ascontiguousarray(xe)
        m["v_hmask"] = np.full((128, 1), 0.0 if h == 0 else 1.0, np.float32)
        m["v_btab"] = np.ascontiguousarray(btab)
        in_maps.append(m)
    return in_maps


def kernel(**inputs):
    in_maps = _stage_inputs(inputs)
    nc = _build()
    res = run_bass_kernel_spmd(nc, in_maps, core_ids=list(range(8)))
    out = np.zeros((4, 4096, D), np.float32)
    for c in range(8):
        b, h = c // 2, c % 2
        out[b, h * TOK:(h + 1) * TOK, :] = np.asarray(res.results[c]["out"]).T
    return out
